# version 01
# baseline (speedup 1.0000x reference)
import contextlib

import numpy as np
import ml_dtypes

import concourse.bass as bass
import concourse.mybir as mybir
from concourse.bass_utils import run_bass_kernel_spmd

F32 = mybir.dt.float32
BF16 = mybir.dt.bfloat16
AF = mybir.ActivationFunctionType
OP = mybir.AluOpType
AX = mybir.AxisListType

NCORES = 8
D = 1024
TILE = 128
NT = 16
NTE = 17
TREAL = NT * TILE
TTOT = NTE * TILE
DIN = 2048
DSTATE = 64
NHEADS = 32
HEADDIM = 64
CONVD = DIN + 2 * DSTATE
DPROJ = 2 * DIN + 2 * DSTATE + NHEADS
NEXP = 64
PSLOT = 32
EXP_LOC = NEXP // NCORES
QCH = 128
NCH = TREAL // QCH
EPS = 1e-5

ENGINES = ("sp", "pe", "act", "dve", "pool")

DEV_STOP = None


class Ctx:

    def __init__(self, nc, sem_stack):
        self.nc = nc
        self.seq = {e: [] for e in ENGINES}
        self.cnt = {}
        self.sem = {}
        self.nbar = 0
        self.dma_sems = set()
        self._sem_stack = sem_stack

    def new_sem(self, name, handle):
        self.sem[name] = handle
        self.cnt[name] = 0

    def on(self, eng, fn):
        self.seq[eng].append(fn)

    def chain(self, eng, emits, inc=None, inc_n=1, wait=()):
        if not isinstance(emits, (list, tuple)):
            emits = [emits]
        sems = self.sem
        tok = None
        if inc is not None:
            self.cnt[inc] += inc_n
            tok = (inc, self.cnt[inc])
        waits = [w for w in wait if w is not None]

        def f(eng_obj, _emits=list(emits), _w=waits, _inc=inc, _n=inc_n):
            for s, v in _w:
                eng_obj.wait_ge(sems[s], v)
            inst = None
            for em in _emits:
                inst = em(eng_obj)
            if _inc is not None:
                inst.then_inc(sems[_inc], _n)

        self.on(eng, f)
        return tok

    def serial_chain(self, eng, emits, inc=None, wait=()):
        tok = None
        for i, em in enumerate(emits):
            tok = self.chain(eng, [em], inc=inc or eng,
                             wait=wait if i == 0 else [tok])
        return tok

    def dma(self, eng, out_ap, in_ap, wait=(), stream=None):
        s = "dma_" + (stream or (eng + "_misc"))
        if s not in self.sem:
            self.new_sem(s, self._sem_stack.enter_context(self.nc.semaphore(s)))
            self.dma_sems.add(s)
        return self.chain(
            eng, lambda e, o=out_ap, i=in_ap: e.dma_start(out=o, in_=i),
            inc=s, inc_n=16, wait=wait)

    def wait(self, eng, *toks):
        sems = self.sem
        toks = [t for t in toks if t is not None]
        if toks:
            self.on(eng, lambda e, _t=list(toks): [e.wait_ge(sems[s], v) for s, v in _t])

    def barrier(self):
        self.nbar += 1
        sems = self.sem
        n = len(ENGINES)
        dma_waits = [(s, self.cnt[s]) for s in sorted(self.dma_sems)
                     if self.cnt[s] > 0]
        for eng in ENGINES:
            waits = list(dma_waits) if eng == "sp" else []
            if eng == "pool" and self.cnt.get("cc", 0) > 0:
                waits.append(("cc", self.cnt["cc"]))

            def f(e, _w=waits, _k=self.nbar, _n=n):
                for s, v in _w:
                    e.wait_ge(sems[s], v)
                e.sem_inc(sems["bar"], 1)
                e.wait_ge(sems["bar"], _k * _n)

            self.on(eng, f)
        self.cnt["bar"] += n


def build_program(dev_stop=None):
    nc = bass.Bass("TRN2", target_bir_lowering=False, debug=False,
                   detect_race_conditions=False)
    es = contextlib.ExitStack()

    def din(name, shape, dtype=F32):
        return nc.dram_tensor(name, list(shape), dtype, kind="ExternalInput").ap()

    x_slice = din("x_slice", [TTOT, D])
    phi_wT = din("phi_wT", [D, DIN])
    phi_b = din("phi_b", [1, DIN])
    W_inT = din("W_inT", [D, DPROJ])
    W_outT = din("W_outT", [DIN, D])
    expw = din("expw", [EXP_LOC, D, D])
    expb_rep = din("expb_rep", [EXP_LOC * PSLOT, D])
    conv_w = din("conv_w", [TILE, 17 * 4])
    conv_b = din("conv_b", [TILE, 17])
    dt_bias = din("dt_bias", [NHEADS, 1])
    A_log = din("A_log", [NHEADS, 1])
    D_row = din("D_row", [1, NHEADS])
    norm_w_rep = din("norm_w_rep", [TILE, DIN])
    ln_w_cols = din("ln_w_cols", [TILE, D // TILE])
    ln_b_cols = din("ln_b_cols", [TILE, D // TILE])
    i_bf = din("i_bf", [TILE, TILE], BF16)
    i_f32 = din("i_f32", [TILE, TILE])
    maskT_in = din("maskT", [TILE, TILE])
    maskneg_in = din("maskneg", [TILE, TILE])
    crows_f = din("crows_f", [1, 4 * TILE])
    crows_b = din("crows_b", [1, TILE], BF16)
    pmask = din("pmask", [TILE, 4])
    oddmask = din("oddmask", [TILE, 1])

    y_out = nc.dram_tensor("y_out", [TREAL, D], F32, kind="ExternalOutput").ap()

    def dint(name, shape, dtype=F32):
        return nc.dram_tensor(name, list(shape), dtype, kind="Internal").ap()

    resid = dint("resid", [TTOT, D])
    x2a_dram = dint("x2a_dram", [TREAL, D])
    eprime = dint("eprime", [16, TILE, TTOT], BF16)
    xbcpre = dint("xbcpre", [NTE, TILE, TREAL + 3], BF16)
    z_dram = dint("z_dram", [TREAL, DIN])
    cc_xt_in = dint("cc_xt_in", [DIN, D])
    cc_xt_out = dint("cc_xt_out", [EXP_LOC * PSLOT, D])
    cc_ag_in = dint("cc_ag_in", [EXP_LOC * PSLOT, D], BF16)
    cc_ag_out = dint("cc_ag_out", [NEXP * PSLOT, D], BF16)
    sloc_dram = dint("sloc_dram", [16, TILE, 1024], BF16)
    cc_st_in = dint("cc_st_in", [TILE, 16 * DSTATE], F32)
    cc_st_out = dint("cc_st_out", [2 * TILE, 16 * DSTATE], F32)

    taps = {}
    if dev_stop == "mamba":
        def tout(name, shape, dtype=F32):
            taps[name] = nc.dram_tensor(name, list(shape), dtype,
                                        kind="ExternalOutput").ap()
        tout("t_x3", [TTOT, D])
        tout("t_ufm", [TILE, 8 * TTOT], BF16)
        tout("t_xbc", [NTE, TILE, TREAL + 3], BF16)
        tout("t_z", [TREAL, DIN])
        tout("t_sg", [NHEADS, TREAL])
        tout("t_vecs", [TILE, 512 * 4 + 256 + 32])
        tout("t_xsf", [TILE, 17 * DIN], BF16)
        tout("t_sloc", [TILE, 16 * 1024], BF16)
        tout("t_x2a", [TREAL, D])
        tout("t_sA", [TILE, 1024])
        tout("t_yg0", [TILE, DIN])
        tout("t_ybf0", [TILE, DIN], BF16)
        tout("t_yfm0", [TILE, DIN], BF16)
    if dev_stop == "ffn1":
        def tout(name, shape, dtype=F32):
            taps[name] = nc.dram_tensor(name, list(shape), dtype,
                                        kind="ExternalOutput").ap()
        tout("t_x2", [TTOT, D])
        tout("t_xln", [TILE, NTE * D], BF16)
        tout("t_dall", [TILE, NTE * DIN], BF16)
        tout("t_e", [TILE, DIN], BF16)
        tout("t_sums", [TILE, 192])
        tout("t_rcr", [TILE, TILE])
        tout("t_eprime", [16, TILE, TTOT], BF16)
        tout("t_xtin", [DIN, D])
        tout("t_xtout", [EXP_LOC * PSLOT, D])
        tout("t_agout", [NEXP * PSLOT, D], BF16)
        tout("t_xt01", [2 * TILE, D])
        tout("t_st01", [2 * TILE, 32])

    c = Ctx(nc, es)
    for s in ["pe", "act", "dve", "pool", "cc", "bar"]:
        c.new_sem(s, es.enter_context(nc.semaphore(s)))

    def SBp(stack, name, shape, dtype=F32):
        return stack.enter_context(nc.sbuf_tensor(name, list(shape), dtype))

    def PSp(stack, name, shape, dtype=F32):
        return stack.enter_context(nc.psum_tensor(name, list(shape), dtype))

    sb_ibf = SBp(es, "c_ibf", [TILE, TILE], BF16)
    sb_if32 = SBp(es, "c_if32", [TILE, TILE])
    sb_maskT = SBp(es, "c_maskT", [TILE, TILE])
    sb_maskneg = SBp(es, "c_maskneg", [TILE, TILE])
    sb_crf = SBp(es, "c_crf", [1, 4 * TILE])
    sb_onescol = SBp(es, "c_onescol", [TILE, 1])
    sb_crb = SBp(es, "c_crb", [1, TILE], BF16)
    sb_phib = SBp(es, "c_phib", [1, DIN], BF16)
    sb_pmask = SBp(es, "c_pmask", [TILE, 4])
    sb_oddmask = SBp(es, "c_oddmask", [TILE, 1])
    ln_w_cols_sb = SBp(es, "c_lnw", [TILE, D // TILE])
    ln_b_cols_sb = SBp(es, "c_lnb", [TILE, D // TILE])
    norm_w_rep_sb = SBp(es, "c_nwrep", [TILE, DIN])
    sb_dp = SBp(es, "c_dp", [1, NHEADS])

    c.chain("pool", [lambda g: g.memset(sb_onescol[:], 1.0)])

    _flt_vals = [EPS, -0.5, -1.0, 0.5, 1.0 / DIN]
    sb_flt = SBp(es, "c_flt", [TILE, len(_flt_vals)])
    for _i, _v in enumerate(_flt_vals):
        _ap = sb_flt[:, _i:_i + 1]
        c.chain("pool", [lambda g, a=_ap, v=_v: g.memset(a, v)])
        nc.const_aps.register(F32, _v, _ap)
    for dst, src, q in [(sb_ibf, i_bf, "sp"), (sb_if32, i_f32, "sp"),
                        (sb_maskT, maskT_in, "sp"),
                        (sb_maskneg, maskneg_in, "sp"),
                        (sb_crf, crows_f, "sp"),
                        (sb_crb, crows_b, "sp"), (sb_pmask, pmask, "sp"),
                        (sb_oddmask, oddmask, "sp"),
                        (ln_w_cols_sb, ln_w_cols, "sp"),
                        (ln_b_cols_sb, ln_b_cols, "sp"),
                        (norm_w_rep_sb, norm_w_rep, "sp"),
                        (sb_dp, D_row, "sp")]:
        c.dma(q, dst[:], src[:], stream="const")
    c.dma("pool", sb_phib[:], phi_b[:], stream="constp")
    c.barrier()

    ST_W = 32
    sb_trash = SBp(es, "c_trash", [TILE, 2048])

    def ln_chain(st, src_ap, width, waits=()):
        s1, sq, msq = st[:, 20:21], st[:, 21:22], st[:, 22:23]
        mu, var = st[:, 24:25], st[:, 25:26]
        lnt, rstd = st[:, 26:27], st[:, 27:28]
        c.chain("dve", [
            lambda v: v.scalar_tensor_tensor(
                sb_trash[:, 0:width], src_ap[:, 0:width], 1.0,
                src_ap[:, 0:width], OP.mult, OP.mult, accum_out=sq),
        ], inc="dve", wait=waits)
        t_acc = c.chain("dve", [
            lambda v: v.reduce_sum(s1, src_ap[:, 0:width], AX.X),
        ], inc="dve")
        t_stats = c.serial_chain("dve", [
            lambda v: v.tensor_scalar_mul(mu, s1, 1.0 / width),
            lambda v: v.tensor_scalar_mul(sq, sq, 1.0 / width),
            lambda v: v.tensor_scalar_mul(msq, mu, mu),
            lambda v: v.tensor_sub(var, sq, msq),
        ], wait=[t_acc])
        t_rstd = c.chain("act", [
            lambda s: s.activation(lnt, var, AF.Ln, bias=EPS),
            lambda s: s.activation(rstd, lnt, AF.Exp, scale=-0.5),
        ], inc="act", wait=[t_stats])
        return mu, rstd, t_rstd

    def emit_ffn(pass_id):
        first = pass_id == 1
        dev1 = (dev_stop == "ffn1" and first)
        tiles = list(range(NTE)) if first else list(range(1, NTE))
        x_src = x_slice if first else resid

        pa = contextlib.ExitStack()
        pfx = f"f{pass_id}"
        sb_xln = SBp(pa, pfx + "xln", [TILE, len(tiles) * D], BF16)
        sb_dall = SBp(pa, pfx + "dall", [TILE, len(tiles) * DIN], BF16)

        paa = contextlib.ExitStack()
        sb_phi = SBp(paa, pfx + "phi", [TILE, 8 * DIN], BF16)
        sb_xt = [SBp(paa, pfx + f"xt{j}", [TILE, D]) for j in range(2)]
        sb_xlf = [SBp(paa, pfx + f"xlf{j}", [TILE, 8 * TILE], BF16) for j in range(2)]
        sb_e = [SBp(paa, pfx + f"e{j}", [TILE, DIN], BF16) for j in range(2)]
        sb_eps = [SBp(paa, pfx + f"eps{j}", [TILE, 16 * TILE], BF16) for j in range(2)]
        sb_st = [SBp(paa, pfx + f"st{j}", [TILE, ST_W]) for j in range(2)]
        sb_sums = [SBp(paa, pfx + f"sums{j}", [TILE, 192]) for j in range(2)]
        sb_rcr = [SBp(paa, pfx + f"rcr{j}", [TILE, TILE]) for j in range(2)]
        ps_lg = PSp(paa, pfx + "pslg", [TILE, DIN])
        ps_tr = PSp(paa, pfx + "pstr", [TILE, 8 * TILE], BF16)
        ps_et = PSp(paa, pfx + "pset", [TILE, 16 * TILE], BF16)
        ps_rc = PSp(paa, pfx + "psrc", [TILE, TILE])

        tok_phi = c.dma("pool", sb_phi.ap().rearrange("p (k n) -> p k n", k=8),
                        phi_wT.rearrange("(k p) n -> p k n", p=TILE),
                        stream="phi")
        c.wait("pe", tok_phi)

        t_xt_rd = [None, None]
        t_xlf_rd = [None, None]
        t_e_rd = [None, None]
        t_eps_sp = [None, None]
        t_rstd_prev = [None, None]
        t_rct_prev = [None, None]
        t_pstr_ev = None
        t_pslg_mx = None
        t_pslg_exp = None
        t_psrc_cp = None
        t_pset_ev = None

        for ii, ti in enumerate(tiles):
            j = ii % 2
            xt, xlf, E, epst = sb_xt[j], sb_xlf[j], sb_e[j], sb_eps[j]
            st, sums, rcr = sb_st[j], sb_sums[j], sb_rcr[j]
            gmaxn = st[:, 28:29]

            tok_x = c.dma("sp", xt[:], x_src[ti * TILE:(ti + 1) * TILE, :],
                          wait=[t_xt_rd[j]], stream=f"x{j}")
            mu, rstd, t_rstd = ln_chain(st, xt, D,
                                        waits=[tok_x, t_rstd_prev[j]])
            t_rstd_prev[j] = t_rstd
            xln_t = sb_xln[:, ii * D:(ii + 1) * D]
            t_ln = c.chain("dve", [
                lambda v, a=xt, o=xln_t, m=mu, r=rstd: v.tensor_scalar(
                    o, a[:], m, r, OP.subtract, OP.mult),
            ], inc="dve", wait=[t_rstd])
            t_xt_rd[j] = t_ln
            if dev1 and ti in (0, 1):
                t1 = c.dma("act", taps["t_xt01"][ti * TILE:(ti + 1) * TILE, :],
                           xt[:], wait=[t_ln], stream="tap")
                t2 = c.dma("act", taps["t_st01"][ti * TILE:(ti + 1) * TILE, :],
                           st[:], wait=[t_ln], stream="tap")
                t_xt_rd[j] = t2
                t_rstd_prev[j] = t2

            t_tr = c.chain("pe", [
                (lambda t, k=k, o=ps_tr, i_=xln_t: t.transpose(
                    o[:, k * TILE:(k + 1) * TILE],
                    i_[:, k * TILE:(k + 1) * TILE], sb_ibf[:]))
                for k in range(8)
            ], inc="pe", wait=[t_ln, t_pstr_ev])
            t_xlf_cp = c.chain("dve", [
                lambda v, o=xlf: v.tensor_copy(o[:], ps_tr[:]),
            ], inc="dve", wait=[t_tr, t_xlf_rd[j]])
            t_pstr_ev = t_xlf_cp

            mm_list = []
            for k in range(8):
                for nb in range(4):
                    mm_list.append(lambda t, k=k, nb=nb, _x=xlf: t.matmul(
                        ps_lg[:, nb * 512:(nb + 1) * 512],
                        _x[:, k * TILE:(k + 1) * TILE],
                        sb_phi[:, k * DIN + nb * 512: k * DIN + (nb + 1) * 512],
                        start=(k == 0), stop=False))
            for nb in range(4):
                mm_list.append(lambda t, nb=nb: t.matmul(
                    ps_lg[:, nb * 512:(nb + 1) * 512], sb_crb[0:1, :],
                    sb_phib[0:1, nb * 512:(nb + 1) * 512],
                    start=False, stop=True))
            t_lgmm = c.chain("pe", mm_list, inc="pe",
                             wait=[t_xlf_cp, t_pslg_mx, t_pslg_exp])
            t_xlf_rd[j] = t_lgmm

            t_mx = c.chain("dve", [
                lambda v, g=gmaxn: v.reduce_max(g, ps_lg[:], AX.X, negate=True),
            ], inc="dve", wait=[t_lgmm])
            t_pslg_mx = t_mx
            t_exp = c.chain("act", [
                lambda s, o=E, g=gmaxn: s.activation(o[:], ps_lg[:], AF.Exp, bias=g),
            ], inc="act", wait=[t_mx, t_e_rd[j]])
            t_pslg_exp = t_exp

            sd, sc = sums[:, 0:64], sums[:, 64:96]
            rd, rc = sums[:, 96:160], sums[:, 160:192]
            dall_t = sb_dall[:, ii * DIN:(ii + 1) * DIN]
            t_red = c.chain("dve", [
                lambda v, a=E, o=sd: v.reduce_sum(
                    o.rearrange("p (n q) -> p n q", q=1),
                    a.ap().rearrange("p (n q) -> p n q", n=64), AX.X),
                lambda v, a=E, o=sc: v.reduce_sum(
                    o.rearrange("p (n q) -> p n q", q=1),
                    a.ap().rearrange("p (n q) -> p n q", n=64).transpose([0, 2, 1]),
                    AX.X),
            ], inc="dve", wait=[t_exp])
            t_rc = c.chain("dve", [
                lambda v, o=rd, i_=sd: v.reciprocal(o, i_),
                lambda v, o=rc, i_=sc: v.reciprocal(o, i_),
            ], inc="dve", wait=[t_red])
            c.chain("dve", [
                lambda v, a=E, o=dall_t, r=rd: v.tensor_tensor(
                    o.rearrange("p (a b) -> p a b", a=64),
                    a.ap().rearrange("p (a b) -> p a b", a=64),
                    r.unsqueeze(2).broadcast_to([TILE, 64, 32]),
                    OP.mult),
            ], wait=[t_rc])
            t_rc4 = c.chain("dve", [
                lambda v, o=rcr, r=rc: v.tensor_copy(
                    o.ap().rearrange("p (a b) -> p a b", a=4),
                    r.unsqueeze(1).broadcast_to([TILE, 4, 32])),
            ], inc="dve", wait=[t_rct_prev[j], t_rc])
            t_rct = c.chain("pe", [
                lambda t, i_=rcr: t.transpose(ps_rc[:], i_[:], sb_if32[:]),
            ], inc="pe", wait=[t_rc4, t_psrc_cp])
            t_rct_prev[j] = t_rct
            t_rcr_cp = c.chain("dve", [
                lambda v, o=rcr: v.tensor_copy(o[:], ps_rc[:]),
            ], inc="dve", wait=[t_rct])
            t_psrc_cp = t_rcr_cp

            t_ett = c.chain("pe", [
                (lambda t, m=m, a=E: t.transpose(
                    ps_et[:, m * TILE:(m + 1) * TILE],
                    a[:, m * TILE:(m + 1) * TILE], sb_ibf[:]))
                for m in range(16)
            ], inc="pe", wait=[t_exp, t_pset_ev])
            t_e_rd[j] = t_ett
            t_epsv = c.chain("dve", [
                lambda v, o=epst, r=rcr: v.tensor_tensor(
                    o.ap().rearrange("p (a b) -> p a b", a=16),
                    ps_et.ap().rearrange("p (a b) -> p a b", a=16),
                    r.ap().unsqueeze(1).broadcast_to([TILE, 16, TILE]),
                    OP.mult),
            ], inc="dve", wait=[t_ett, t_eps_sp[j], t_rcr_cp])
            t_pset_ev = t_epsv
            t_eps_sp[j] = c.dma(
                "sp", eprime[:, :, ti * TILE:(ti + 1) * TILE].transpose([1, 0, 2]),
                epst.ap().rearrange("p (m t) -> p m t", m=16), wait=[t_epsv],
                stream=f"epsp{j}")
            if dev1 and ti == 3:
                c.dma("act", taps["t_e"][:], E[:], wait=[t_exp], stream="tap")
                c.dma("act", taps["t_sums"][:], sums[:], wait=[t_rc], stream="tap")
                c.dma("act", taps["t_rcr"][:], rcr[:], wait=[t_rcr_cp], stream="tap")

        c.barrier()
        if dev1:
            c.dma("act", taps["t_xln"][:], sb_xln[:], stream="tap")
            c.dma("act", taps["t_dall"][:], sb_dall[:], stream="tap")
            c.dma("act", taps["t_eprime"][:], eprime[:], stream="tap")
        paa.close()

        pb = contextlib.ExitStack()
        ps_x2 = [PSp(pb, pfx + f"psxt{j}", [TILE, D]) for j in range(2)]
        sb_xp = [SBp(pb, pfx + f"xp{j}", [TILE, D]) for j in range(2)]
        real_ii = [ii for ii, t0 in enumerate(tiles) if t0 >= 1]
        t_x2ev = [None, None]
        t_xp_wr = [None, None]
        for m in range(16):
            j = m % 2
            mm_list = []
            for kk, ii in enumerate(real_ii):
                for nb in range(2):
                    mm_list.append(lambda t, m=m, ii=ii, nb=nb, j=j, kk=kk: t.matmul(
                        ps_x2[j][:, nb * 512:(nb + 1) * 512],
                        sb_dall[:, ii * DIN + m * TILE: ii * DIN + (m + 1) * TILE],
                        sb_xln[:, ii * D + nb * 512: ii * D + (nb + 1) * 512],
                        start=(kk == 0), stop=(kk == len(real_ii) - 1)))
            t_mm = c.chain("pe", mm_list, inc="pe", wait=[t_x2ev[j]])
            t_ev = c.chain("dve", [
                lambda v, j=j: v.tensor_copy(sb_xp[j][:], ps_x2[j][:]),
            ], inc="dve", wait=[t_mm, t_xp_wr[j]])
            t_x2ev[j] = t_ev
            t_xp_wr[j] = c.dma("sp", cc_xt_in[m * TILE:(m + 1) * TILE, :],
                               sb_xp[j][:], wait=[t_ev], stream=f"xpst{j}")
        c.barrier()
        pb.close()
        pa.close()
        if dev1:
            c.dma("act", taps["t_xtin"][:], cc_xt_in[:], stream="tap")

        t_rs = c.chain("pool", [lambda g: g.collective_compute(
            "ReduceScatter", OP.add,
            replica_groups=[[0, 1, 2, 3, 4, 5, 6, 7]],
            ins=[cc_xt_in[:]], outs=[cc_xt_out[:]])], inc="cc")
        if dev1:
            c.dma("act", taps["t_xtout"][:], cc_xt_out[:], wait=[t_rs], stream="tap")

        pc = contextlib.ExitStack()
        sb_xts = SBp(pc, pfx + "xts", [TILE, 2 * D])
        sb_xtb = SBp(pc, pfx + "xtb", [TILE, 2 * D], BF16)
        sb_xtf = SBp(pc, pfx + "xtf", [TILE, 8 * 256], BF16)
        sb_ew = [SBp(pc, pfx + f"ew{j}", [TILE, 8 * D], BF16) for j in range(2)]
        sb_ytb = SBp(pc, pfx + "ytb", [TILE, 2 * D])
        sb_ytbb = SBp(pc, pfx + "ytbb", [TILE, 2 * D], BF16)
        sb_eb = SBp(pc, pfx + "eb", [TILE, 2 * D])
        sb_st4 = SBp(pc, pfx + "st4", [TILE, ST_W])
        ps_e = PSp(pc, pfx + "pse", [TILE, D])
        ps_xtt = PSp(pc, pfx + "psxtt", [TILE, 8 * 256], BF16)

        tok_xt = c.dma("sp", sb_xts.ap().rearrange("p (a d) -> p a d", a=2),
                       cc_xt_out.rearrange("(a p) d -> p a d", p=TILE),
                       wait=[t_rs], stream="xtrs")
        tok_eb = c.dma("sp", sb_eb.ap().rearrange("p (a d) -> p a d", a=2),
                       expb_rep.rearrange("(a p) d -> p a d", p=TILE),
                       stream="eb")
        tok_ew = [None] * EXP_LOC
        tok_ew[0] = c.dma("pool",
                          sb_ew[0].ap().rearrange("p (k h) -> p k h", k=8),
                          expw[0].rearrange("(k p) h -> p k h", p=TILE),
                          stream="ew0")

        t_clip = c.chain("dve", [lambda v: v.tensor_scalar(
            sb_xts[:], sb_xts[:], -33000.0, 65000.0, OP.max, OP.min)],
            inc="dve", wait=[tok_xt])
        t_tanh = None
        for a in range(2):
            mu4, rstd4, t_r = ln_chain(sb_st4, sb_xts[:, a * D:(a + 1) * D], D,
                                       waits=[t_clip, t_tanh])
            t_ap = c.chain("dve", [lambda v, a=a, m=mu4, r=rstd4: v.tensor_scalar(
                sb_xts[:, a * D:(a + 1) * D], sb_xts[:, a * D:(a + 1) * D],
                m, r, OP.subtract, OP.mult)], inc="dve", wait=[t_r])
            t_tanh = c.chain("act", [lambda s, a=a: s.activation(
                sb_xtb[:, a * D:(a + 1) * D], sb_xts[:, a * D:(a + 1) * D],
                AF.Tanh)], inc="act", wait=[t_ap])

        t_xtt = c.chain("pe", [
            (lambda t, a=a, k=k: t.transpose(
                ps_xtt[:, k * 256 + a * TILE: k * 256 + (a + 1) * TILE],
                sb_xtb[:, a * D + k * TILE: a * D + (k + 1) * TILE],
                sb_ibf[:]))
            for a in range(2) for k in range(8)
        ], inc="pe", wait=[t_tanh])
        t_xtf = c.chain("dve", [lambda v: v.tensor_copy(sb_xtf[:], ps_xtt[:])],
                        inc="dve", wait=[t_xtt])

        t_eg_prev = None
        t_mm_exp = [None] * EXP_LOC
        for g4 in range(2):
            for nl in range(4):
                n = g4 * 4 + nl
                if n + 1 < EXP_LOC:
                    tok_ew[n + 1] = c.dma(
                        "pool",
                        sb_ew[(n + 1) % 2].ap().rearrange("p (k h) -> p k h", k=8),
                        expw[n + 1].rearrange("(k p) h -> p k h", p=TILE),
                        wait=[t_mm_exp[n - 1]] if n >= 1 else [],
                        stream=f"ew{(n + 1) % 2}")
                mm_list = []
                for k in range(8):
                    for nb in range(2):
                        mm_list.append(lambda t, n=n, nl=nl, k=k, nb=nb: t.matmul(
                            ps_e[nl * 32:(nl + 1) * 32, nb * 512:(nb + 1) * 512],
                            sb_xtf[:, k * 256 + n * 32: k * 256 + n * 32 + 32],
                            sb_ew[n % 2][:, k * D + nb * 512: k * D + (nb + 1) * 512],
                            start=(k == 0), stop=(k == 7),
                            tile_position=(0, nl * 32)))
                t_mm_exp[n] = c.chain("pe", mm_list, inc="pe",
                                      wait=[t_xtf, tok_ew[n], t_eg_prev])
            t_eg_prev = c.chain("dve", [lambda v, g4=g4: v.tensor_copy(
                sb_ytb[:, g4 * D:(g4 + 1) * D], ps_e[:])], inc="dve",
                wait=[t_mm_exp[g4 * 4 + 3]])
        t_bias = c.chain("dve", [lambda v: v.tensor_tensor(
            sb_ytbb[:], sb_ytb[:], sb_eb[:], OP.add)], inc="dve",
            wait=[tok_eb, t_eg_prev])
        c.dma("sp", cc_ag_in.rearrange("(a p) d -> p a d", p=TILE),
              sb_ytbb.ap().rearrange("p (a d) -> p a d", a=2), wait=[t_bias],
              stream="agst")
        c.barrier()
        pc.close()

        t_ag = c.chain("pool", [lambda g: g.collective_compute(
            "AllGather", OP.bypass,
            replica_groups=[[0, 1, 2, 3, 4, 5, 6, 7]],
            ins=[cc_ag_in[:]], outs=[cc_ag_out[:]])], inc="cc")
        if dev1:
            c.dma("act", taps["t_agout"][:], cc_ag_out[:], wait=[t_ag], stream="tap")

        pd = contextlib.ExitStack()
        sb_yt = SBp(pd, pfx + "yt", [TILE, 16 * D], BF16)
        sb_ep = [SBp(pd, pfx + f"ep{j}", [TILE, 16 * TILE], BF16) for j in range(2)]
        sb_x = [SBp(pd, pfx + f"x{j}", [TILE, D]) for j in range(2)]
        sb_x2 = [SBp(pd, pfx + f"x2{j}", [TILE, D]) for j in range(2)]
        ps_y = [PSp(pd, pfx + f"psy{j}", [TILE, D]) for j in range(2)]

        tok_yt = c.dma("sp", sb_yt.ap().rearrange("p (m h) -> p m h", m=16),
                       cc_ag_out.rearrange("(m p) h -> p m h", p=TILE),
                       wait=[t_ag], stream="yt")
        c.wait("pe", tok_yt)
        t_ep_rd = [None, None]
        t_x_rd = [None, None]
        t_yev = [None, None]
        t_x2_wr = [None, None]
        for ii, ti in enumerate(tiles):
            j = ii % 2
            tok_ep = c.dma("sp", sb_ep[j].ap().rearrange("p (m t) -> p m t", m=16),
                           eprime[:, :, ti * TILE:(ti + 1) * TILE].transpose([1, 0, 2]),
                           wait=[t_ep_rd[j]], stream=f"ep{j}")
            tok_x = c.dma("sp", sb_x[j][:], x_src[ti * TILE:(ti + 1) * TILE, :],
                          wait=[t_x_rd[j]], stream=f"x6{j}")
            mm_list = []
            for k in range(16):
                for nb in range(2):
                    mm_list.append(lambda t, j=j, k=k, nb=nb: t.matmul(
                        ps_y[j][:, nb * 512:(nb + 1) * 512],
                        sb_ep[j][:, k * TILE:(k + 1) * TILE],
                        sb_yt[:, k * D + nb * 512: k * D + (nb + 1) * 512],
                        start=(k == 0), stop=(k == 15)))
            t_ymm = c.chain("pe", mm_list, inc="pe", wait=[tok_ep, t_yev[j]])
            t_ep_rd[j] = t_ymm
            t_x2 = c.chain("dve", [lambda v, j=j: v.scalar_tensor_tensor(
                sb_x2[j][:], ps_y[j][:], 0.5, sb_x[j][:], OP.mult, OP.add)],
                inc="dve", wait=[t_ymm, tok_x, t_x2_wr[j]])
            t_x_rd[j] = t_x2
            t_yev[j] = t_x2
            t_wr = c.dma("sp", resid[ti * TILE:(ti + 1) * TILE, :], sb_x2[j][:],
                         wait=[t_x2], stream=f"x2st{j}")
            if not first:
                t_wr = c.dma("sp", y_out[(ti - 1) * TILE:ti * TILE, :],
                             sb_x2[j][:], wait=[t_x2], stream=f"x2st{j}")
            t_x2_wr[j] = t_wr
        c.barrier()
        pd.close()

    def emit_mamba():
        pfx = "m_"
        mes = contextlib.ExitStack()
        sb_dtb = SBp(mes, pfx + "dtb", [NHEADS, 4])
        sb_scol = SBp(mes, pfx + "scol", [TILE, 512])
        sb_bcol = SBp(mes, pfx + "bcol", [TILE, 512])
        sb_escol = SBp(mes, pfx + "escol", [TILE, 512])
        sb_wcol = SBp(mes, pfx + "wcolb", [TILE, 512], BF16)
        sb_lam = SBp(mes, pfx + "lamall", [TILE, 256])
        sb_slast = SBp(mes, pfx + "slast", [1, 512])
        sb_lamrow = SBp(mes, pfx + "lamrow", [1, 512])
        sb_dtm = SBp(mes, pfx + "dtm", [TILE, NHEADS])
        m12 = contextlib.ExitStack()
        sb_sg = SBp(m12, pfx + "sg", [NHEADS, TREAL])
        sb_ufm = SBp(m12, pfx + "ufm", [TILE, 8 * TTOT], BF16)

        m1 = contextlib.ExitStack()
        sb_x2t = [SBp(m1, pfx + f"x2t{j}", [TILE, D]) for j in range(2)]
        sb_xa = [SBp(m1, pfx + f"xa{j}", [TILE, D]) for j in range(2)]
        sb_x2a = [SBp(m1, pfx + f"x2a{j}", [TILE, D]) for j in range(2)]
        sb_up = [SBp(m1, pfx + f"up{j}", [TILE, D], BF16) for j in range(2)]
        sb_mst = [SBp(m1, pfx + f"mst{j}", [TILE, ST_W]) for j in range(2)]
        sb_mst2 = [SBp(m1, pfx + f"mst2{j}", [TILE, ST_W]) for j in range(2)]
        ps_utr = PSp(m1, pfx + "psutr", [TILE, 8 * TILE], BF16)

        t_x2_rd = [None, None]
        t_rp1 = [None, None]
        t_rp2 = [None, None]
        t_x2a_sp = [None, None]
        t_ufm_wr = None
        t_psutr_ev = None
        for ti in range(NTE):
            j = ti % 2
            x2t, xa, x2a, up = sb_x2t[j], sb_xa[j], sb_x2a[j], sb_up[j]
            tok = c.dma("sp", x2t[:], resid[ti * TILE:(ti + 1) * TILE, :],
                        wait=[t_x2_rd[j]], stream=f"mx{j}")
            mu1, rstd1, t_r1 = ln_chain(sb_mst[j], x2t, D,
                                        waits=[tok, t_rp1[j]])
            t_rp1[j] = t_r1
            t_xa = c.chain("dve", [lambda v, o=xa, a=x2t, m=mu1, r=rstd1:
                                   v.tensor_scalar(o[:], a[:], m, r,
                                                   OP.subtract, OP.mult)],
                           inc="dve", wait=[t_r1])
            if ti >= 1:
                t_x2av = c.chain("dve", [lambda v, o=x2a, a=x2t, b=xa:
                                         v.tensor_add(o[:], a[:], b[:])],
                                 inc="dve", wait=[t_x2a_sp[j]])
                t_x2a_sp[j] = c.dma(
                    "sp", x2a_dram[(ti - 1) * TILE:ti * TILE, :], x2a[:],
                    wait=[t_x2av], stream=f"x2asp{j}")
            mu2, rstd2, t_r2 = ln_chain(sb_mst2[j], xa, D,
                                        waits=[t_xa, t_rp2[j]])
            t_rp2[j] = t_r2
            t_up = c.chain("dve", [lambda v, o=up, a=xa, m=mu2, r=rstd2:
                                   v.tensor_scalar(o[:], a[:], m, r,
                                                   OP.subtract, OP.mult)],
                           inc="dve", wait=[t_r2])
            t_x2_rd[j] = t_up
            t_tr = c.chain("pe", [
                (lambda t, k=k, i_=up: t.transpose(
                    ps_utr[:, k * TILE:(k + 1) * TILE],
                    i_[:, k * TILE:(k + 1) * TILE], sb_ibf[:]))
                for k in range(8)
            ], inc="pe", wait=[t_up, t_psutr_ev])
            ev = []
            for k in range(8):
                ev.append(lambda v, k=k, ti=ti: v.scalar_tensor_tensor(
                    sb_ufm[:, k * TTOT + ti * TILE: k * TTOT + (ti + 1) * TILE],
                    ps_utr[:, k * TILE:(k + 1) * TILE],
                    ln_w_cols_sb[:, k:k + 1],
                    ln_b_cols_sb[:, k:k + 1].broadcast_to([TILE, TILE]),
                    OP.mult, OP.add))
            t_ufm_wr = c.chain("dve", ev, inc="dve", wait=[t_tr])
            t_psutr_ev = t_ufm_wr
        t_pm = c.chain("dve", [
            (lambda v, k=k: v.tensor_tensor(
                sb_ufm[:, k * TTOT + 125: k * TTOT + 128],
                sb_ufm[:, k * TTOT + 125: k * TTOT + 128],
                sb_pmask[:, 0:3], OP.mult))
            for k in range(8)
        ], inc="dve", wait=[t_ufm_wr])
        c.barrier()
        if dev_stop == "mamba":
            c.dma("act", taps["t_ufm"][:], sb_ufm[:], stream="tap")
        m1.close()

        m2 = contextlib.ExitStack()
        sb_wz = SBp(m2, pfx + "wz", [TILE, 8 * DIN], BF16)
        sb_zs = [SBp(m2, pfx + f"zs{j}", [TILE, DIN]) for j in range(2)]
        tok_wz = c.dma("pool", sb_wz.ap().rearrange("p (k n) -> p k n", k=8),
                       W_inT[:, 0:DIN].rearrange("(k p) n -> p k n", p=TILE),
                       stream="wz")
        m2a = contextlib.ExitStack()
        ps_z = PSp(m2a, pfx + "psz", [TILE, DIN])
        t_zev = [None, None]
        t_zsp = [None, None]
        t_zmm_prev = None
        c.wait("pe", tok_wz)
        for ti in range(1, NTE):
            j = ti % 2
            mm_list = []
            for k in range(8):
                for nb in range(4):
                    mm_list.append(lambda t, ti=ti, k=k, nb=nb: t.matmul(
                        ps_z[:, nb * 512:(nb + 1) * 512],
                        sb_ufm[:, k * TTOT + ti * TILE: k * TTOT + (ti + 1) * TILE],
                        sb_wz[:, k * DIN + nb * 512: k * DIN + (nb + 1) * 512],
                        start=(k == 0), stop=(k == 7)))
            t_zmm = c.chain("pe", mm_list, inc="pe", wait=[t_zmm_prev])
            t_zev[j] = c.chain("act", [lambda s, j=j: s.activation(
                sb_zs[j][:], ps_z[:], AF.Silu)], inc="act",
                wait=[t_zmm, t_zsp[j]])
            t_zmm_prev = t_zev[j]
            t_zsp[j] = c.dma("sp", z_dram[(ti - 1) * TILE:ti * TILE, :],
                             sb_zs[j][:], wait=[t_zev[j]], stream=f"zsp{j}")
        c.barrier()
        m2a.close()

        m2b = contextlib.ExitStack()
        sb_wxbc = [SBp(m2b, pfx + f"wxbc{j}", [TILE, 8 * TILE], BF16)
                   for j in range(2)]
        sb_xbcst = [SBp(m2b, pfx + f"xbcst{j}", [TILE, TTOT], BF16)
                    for j in range(2)]
        ps_xbc = PSp(m2b, pfx + "psxbc", [TILE, 2560])
        TBW = [(0, 512), (512, 512), (1024, 512), (1536, 512), (2048, 128)]
        tok_wx = [None, None]
        tok_wx[0] = c.dma("pool",
                          sb_wxbc[0].ap().rearrange("p (k n) -> p k n", k=8),
                          W_inT[:, DIN:DIN + TILE].rearrange("(k p) n -> p k n",
                                                             p=TILE),
                          stream="wxbc0")
        t_xbev = [None, None]
        t_xbev_last = None
        t_xbsp = [None, None]
        t_xbmm = [None, None]
        for mi in range(17):
            j = mi % 2
            if mi + 1 < 17:
                tok_wx[(mi + 1) % 2] = c.dma(
                    "pool",
                    sb_wxbc[(mi + 1) % 2].ap().rearrange("p (k n) -> p k n", k=8),
                    W_inT[:, DIN + (mi + 1) * TILE: DIN + (mi + 2) * TILE
                          ].rearrange("(k p) n -> p k n", p=TILE),
                    wait=[t_xbmm[(mi + 1) % 2]], stream=f"wxbc{(mi + 1) % 2}")
            mm_list = []
            for tb, (c0, w) in enumerate(TBW):
                for k in range(8):
                    mm_list.append(lambda t, mi=mi, j=j, k=k, c0=c0, w=w, tb=tb:
                                   t.matmul(
                        ps_xbc[:, tb * 512: tb * 512 + w],
                        sb_wxbc[j][:, k * TILE:(k + 1) * TILE],
                        sb_ufm[:, k * TTOT + c0: k * TTOT + c0 + w],
                        start=(k == 0), stop=(k == 7)))
            t_mm = c.chain("pe", mm_list, inc="pe",
                           wait=[tok_wx[j], t_xbev_last, t_pm])
            t_xbmm[j] = t_mm
            t_ev = c.chain("dve", [lambda v, j=j: v.tensor_copy(
                sb_xbcst[j][:], ps_xbc[:, 0:TTOT])], inc="dve",
                wait=[t_mm, t_xbsp[j]])
            t_xbev[j] = t_ev
            t_xbev_last = t_ev
            t_xbsp[j] = c.dma("sp", xbcpre[mi, :, :],
                              sb_xbcst[j][:, 125:TTOT], wait=[t_ev],
                              stream=f"xbcsp{j}")
        c.barrier()
        m2b.close()

        m2c = contextlib.ExitStack()
        sb_wdt = SBp(m2c, pfx + "wdt", [TILE, 8 * 32], BF16)
        ps_dt = PSp(m2c, pfx + "psdt", [32, 512])
        tok_wdt = c.dma("pool", sb_wdt.ap().rearrange("p (k n) -> p k n", k=8),
                        W_inT[:, DPROJ - 32:DPROJ].rearrange("(k p) n -> p k n",
                                                             p=TILE),
                        stream="wdt")
        tok_dtb = c.dma("sp", sb_dtb[:, 0:1], dt_bias[:], stream="dtb")
        tok_alog = c.dma("sp", sb_dtb[:, 1:2], A_log[:], stream="dtb")
        t_ndtb = c.chain("dve", [lambda v: v.tensor_scalar_mul(
            sb_dtb[:, 2:3], sb_dtb[:, 0:1], -1.0)], inc="dve",
            wait=[tok_dtb, tok_alog])
        t_eA = c.chain("act", [lambda s: s.activation(
            sb_dtb[:, 3:4], sb_dtb[:, 1:2], AF.Exp)], inc="act", wait=[t_ndtb])
        t_sg_prev = None
        c.wait("pe", tok_wdt)
        for tb in range(4):
            mm_list = []
            for k in range(8):
                mm_list.append(lambda t, tb=tb, k=k: t.matmul(
                    ps_dt[:, 0:512],
                    sb_wdt[:, k * 32:(k + 1) * 32],
                    sb_ufm[:, k * TTOT + TILE + tb * 512:
                           k * TTOT + TILE + (tb + 1) * 512],
                    start=(k == 0), stop=(k == 7)))
            t_mm = c.chain("pe", mm_list, inc="pe", wait=[t_sg_prev])
            t_sg_prev = c.chain("act", [lambda s, tb=tb: s.activation(
                sb_sg[:, tb * 512:(tb + 1) * 512], ps_dt[:], AF.Sigmoid,
                scale=-1.0, bias=sb_dtb[:, 2:3])], inc="act",
                wait=[t_mm, t_ndtb])
        c.barrier()
        m2c.close()
        m2.close()
        if dev_stop == "mamba":
            c.dma("act", taps["t_xbc"][:], xbcpre[:], stream="tap")
            c.dma("act", taps["t_z"][:], z_dram[:], stream="tap")
            c.dma("act", taps["t_sg"][:], sb_sg[:], stream="tap")
            c.dma("act", taps["t_x2a"][:], x2a_dram[:], stream="tap")

        m3 = contextlib.ExitStack()
        sb_l = SBp(m3, pfx + "l", [NHEADS, TREAL])
        sb_dtA = SBp(m3, pfx + "dtA", [NHEADS, TREAL])
        sb_lndt = SBp(m3, pfx + "lndt", [NHEADS, TREAL])
        sb_dtAc = SBp(m3, pfx + "dtAc", [TILE, 512])
        sb_lndtc = SBp(m3, pfx + "lndtc", [TILE, 512])
        ps_m3 = PSp(m3, pfx + "psm3", [TILE, 512])
        ps_m3b = PSp(m3, pfx + "psm3b", [TILE, 512])

        t_l = c.chain("act", [lambda s: s.activation(sb_l[:], sb_sg[:], AF.Ln)],
                      inc="act")
        t_dtA = c.chain("dve", [lambda v: v.tensor_scalar_mul(
            sb_dtA[:], sb_l[:], sb_dtb[:, 3:4])], inc="dve",
            wait=[t_l, t_eA])
        t_lndt = c.chain("act", [lambda s: s.activation(
            sb_lndt[:], sb_l[:], AF.Ln, scale=-1.0)], inc="act", wait=[t_dtA])
        t_tr1 = c.chain("pe", [
            (lambda t, cc=cc: t.transpose(
                ps_m3[:, cc * 32:(cc + 1) * 32],
                sb_dtA[:, cc * TILE:(cc + 1) * TILE], sb_if32[0:32, 0:32]))
            for cc in range(16)
        ], inc="pe", wait=[t_dtA])
        t_dtac = c.chain("dve", [lambda v: v.tensor_copy(sb_dtAc[:], ps_m3[:])],
                         inc="dve", wait=[t_tr1])
        t_tr2 = c.chain("pe", [
            (lambda t, cc=cc: t.transpose(
                ps_m3b[:, cc * 32:(cc + 1) * 32],
                sb_lndt[:, cc * TILE:(cc + 1) * TILE], sb_if32[0:32, 0:32]))
            for cc in range(16)
        ], inc="pe", wait=[t_lndt])
        t_lndc = c.chain("dve", [lambda v: v.tensor_copy(sb_lndtc[:], ps_m3b[:])],
                         inc="dve", wait=[t_tr2])
        t_mm_s = c.chain("pe", [
            lambda t: t.matmul(ps_m3[:], sb_maskT[:], sb_dtAc[:],
                               start=True, stop=True)],
            inc="pe", wait=[t_dtac])
        t_scol = c.chain("dve", [lambda v: v.tensor_copy(sb_scol[:], ps_m3[:])],
                         inc="dve", wait=[t_mm_s])
        t_mm_sl = c.chain("pe", [
            lambda t: t.matmul(ps_m3b[0:1, :], sb_onescol[:], sb_dtAc[:],
                               start=True, stop=True)],
            inc="pe", wait=[t_scol])
        t_slast = c.chain("dve", [lambda v: v.tensor_copy(
            sb_slast[:], ps_m3b[0:1, :])], inc="dve", wait=[t_mm_sl])
        t_bcol = c.chain("dve", [lambda v: v.tensor_sub(
            sb_bcol[:], sb_lndtc[:], sb_scol[:])], inc="dve",
            wait=[t_lndc, t_scol])
        t_escol = c.chain("act", [lambda s: s.activation(
            sb_escol[:], sb_scol[:], AF.Exp)], inc="act", wait=[t_scol])
        t_lamr = c.chain("act", [lambda s: s.activation(
            sb_lamrow[:], sb_slast[:], AF.Exp)], inc="act", wait=[t_slast])
        t_mm_w = c.chain("pe", [
            lambda t: t.matmul(ps_m3[:], sb_crf[0:1, 0:TILE], sb_slast[:],
                               start=True, stop=False),
            lambda t: t.matmul(ps_m3[:], sb_if32[:], sb_bcol[:],
                               start=False, stop=True)],
            inc="pe", wait=[t_slast, t_bcol, t_scol])
        t_wcol = c.chain("act", [lambda s: s.activation(
            sb_wcol[:], ps_m3[:], AF.Exp)], inc="act", wait=[t_mm_w])
        lam_even = sb_lamrow.ap().rearrange("o (cc h) -> o cc h", h=32)[:, :, 0:16]
        lam_odd = sb_lamrow.ap().rearrange("o (cc h) -> o cc h", h=32)[:, :, 16:32]
        t_mm_lam = c.chain("pe", [
            lambda t: t.matmul(ps_m3b[:, 0:256], sb_crf[0:1, TILE:2 * TILE],
                               lam_even, start=True, stop=False),
            lambda t: t.matmul(ps_m3b[:, 0:256], sb_crf[0:1, 2 * TILE:3 * TILE],
                               lam_odd, start=False, stop=True)],
            inc="pe", wait=[t_lamr])
        t_lam = c.chain("dve", [lambda v: v.tensor_copy(
            sb_lam[:], ps_m3b[:, 0:256])], inc="dve", wait=[t_mm_lam])
        t_mm_d = c.chain("pe", [
            lambda t: t.matmul(ps_m3[:, 0:NHEADS], sb_crf[0:1, 0:TILE],
                               sb_dp[:], start=True, stop=True)],
            inc="pe", wait=[t_wcol])
        t_dcol = c.chain("dve", [lambda v: v.tensor_copy(
            sb_dtm[:], ps_m3[:, 0:NHEADS])], inc="dve", wait=[t_mm_d])
        c.barrier()
        if dev_stop == "mamba":
            c.dma("act", taps["t_vecs"][:, 0:512], sb_scol[:], stream="tap")
            c.dma("act", taps["t_vecs"][:, 512:1024], sb_bcol[:], stream="tap")
            c.dma("act", taps["t_vecs"][:, 1024:1536], sb_escol[:], stream="tap")
            c.chain("dve", [lambda v: v.tensor_copy(
                sb_trash[:, 0:512], sb_wcol[:])], inc="dve")
            c.dma("act", taps["t_vecs"][:, 1536:2048], sb_trash[:, 0:512],
                  wait=[("dve", c.cnt["dve"])], stream="tap")
            c.dma("act", taps["t_vecs"][:, 2048:2304], sb_lam[:], stream="tap")
            c.dma("act", taps["t_vecs"][:, 2304:2336], sb_dtm[:], stream="tap")
        m3.close()
        m12.close()

        m4 = contextlib.ExitStack()
        sb_xsf = SBp(m4, pfx + "xsf", [TILE, 17 * DIN], BF16)
        sb_c0b = SBp(m4, pfx + "c0b", [DSTATE, TREAL], BF16)
        sb_cw = SBp(m4, pfx + "cw", [TILE, 17 * 4])
        sb_cb = SBp(m4, pfx + "cb", [TILE, 17])
        sb_xstm = [SBp(m4, pfx + f"xstm{j}", [TILE, DIN + DSTATE], BF16)
                   for j in range(2)]
        sb_slocb = [SBp(m4, pfx + f"slocb{j}", [TILE, 1024], BF16)
                    for j in range(2)]
        sb_sA = [SBp(m4, pfx + f"sA{j}", [TILE, 1024]) for j in range(2)]
        sb_stmp = SBp(m4, pfx + "stmp", [TILE, 1024])
        m4p = contextlib.ExitStack()
        sb_xbr = [SBp(m4p, pfx + f"xbr{j}", [TILE, TREAL + 3], BF16)
                  for j in range(2)]
        sb_acc = [SBp(m4p, pfx + "acc0", [TILE, TREAL])] * 2
        tok_cw = c.dma("sp", sb_cw[:], conv_w[:], stream="cw")
        tok_cb = c.dma("sp", sb_cb[:], conv_b[:], stream="cw")
        t_xbr_rd = [None, None]
        t_acc_rd = [None, None]
        t_conv_done = []
        for r in range(17):
            j = r % 2
            tok_x = c.dma("sp", sb_xbr[j][:], xbcpre[r, :, :],
                          wait=[t_xbr_rd[j]], stream=f"xbr{j}")
            ops = [lambda v, j=j, r=r: v.tensor_scalar_mul(
                sb_acc[j][:], sb_xbr[j][:, 0:TREAL], sb_cw[:, r * 4:r * 4 + 1])]
            for k in range(1, 4):
                ops.append(lambda v, j=j, r=r, k=k: v.scalar_tensor_tensor(
                    sb_acc[j][:], sb_xbr[j][:, k:TREAL + k],
                    sb_cw[:, r * 4 + k:r * 4 + k + 1], sb_acc[j][:],
                    OP.mult, OP.add))
            t_cv = c.chain("dve", ops, inc="dve",
                           wait=[tok_x, tok_cw, t_acc_rd[0]])
            t_xbr_rd[j] = t_cv
            t_si = c.chain("act", [lambda s, j=j, r=r: s.activation(
                sb_xsf[:, r * DIN:(r + 1) * DIN], sb_acc[j][:], AF.Silu,
                bias=sb_cb[:, r:r + 1])], inc="act",
                wait=[t_cv, tok_cb])
            t_acc_rd[0] = t_si
            t_conv_done.append(t_si)
        t_c0b = c.dma("sp", sb_c0b[:],
                      sb_xsf[DSTATE:2 * DSTATE, 16 * DIN:17 * DIN],
                      wait=[t_conv_done[16]], stream="ccp")
        c.barrier()
        m4p.close()
        if dev_stop == "mamba":
            c.dma("act", taps["t_xsf"][:], sb_xsf[:], stream="tap")

        m4a = contextlib.ExitStack()
        sb_xw = [SBp(m4a, pfx + f"xw{j}", [TILE, DIN], BF16) for j in range(2)]
        ps_xta = PSp(m4a, pfx + "psxta", [TILE, DIN + DSTATE], BF16)
        ps_S = PSp(m4a, pfx + "psS", [TILE, 1024])

        t_init0 = c.chain("pool", [lambda g: g.memset(sb_sA[0][:], 0.0)],
                          inc="pool")
        t_slsp = [None, None]
        t_xstm_rd = [None, None]
        t_xw_rd = [None, None]
        t_psxta_ev = None
        t_psS_rd = None
        t_rec = t_init0
        for cc in range(16):
            j = cc % 2
            trs = [
                (lambda t, r=r, cc=cc: t.transpose(
                    ps_xta[:, r * TILE:(r + 1) * TILE],
                    sb_xsf[:, r * DIN + cc * TILE: r * DIN + (cc + 1) * TILE],
                    sb_ibf[:]))
                for r in range(16)
            ]
            trs.append(lambda t, cc=cc: t.transpose(
                ps_xta[:, DIN:DIN + DSTATE],
                sb_xsf[0:DSTATE, 16 * DIN + cc * TILE: 16 * DIN + (cc + 1) * TILE],
                sb_ibf[0:DSTATE, 0:DSTATE]))
            t_tr = c.chain("pe", trs, inc="pe",
                           wait=[t_conv_done[-1], t_psxta_ev])
            t_ev = c.chain("dve", [lambda v, j=j: v.tensor_copy(
                sb_xstm[j][:], ps_xta[:])], inc="dve",
                wait=[t_tr, t_xstm_rd[j]])
            t_psxta_ev = t_ev
            t_xwv = c.chain("dve", [lambda v, j=j, cc=cc: v.tensor_tensor(
                sb_xw[j].ap().rearrange("p (h q) -> p h q", h=32),
                sb_xstm[j][:, 0:DIN].rearrange("p (h q) -> p h q", h=32),
                sb_wcol[:, cc * 32:(cc + 1) * 32].unsqueeze(2
                    ).broadcast_to([TILE, 32, HEADDIM]),
                OP.mult)], inc="dve", wait=[t_ev, t_xw_rd[j], t_wcol])
            mm_list = []
            for h in range(NHEADS):
                h2, hp = h // 16, h % 16
                mm_list.append(lambda t, j=j, h2=h2, hp=hp, h=h: t.matmul(
                    ps_S[h2 * 64:(h2 + 1) * 64, hp * 64:(hp + 1) * 64],
                    sb_xstm[j][:, DIN:DIN + DSTATE],
                    sb_xw[j][:, h * HEADDIM:(h + 1) * HEADDIM],
                    start=True, stop=True,
                    tile_position=(0, 64 * h2)))
            t_smm = c.chain("pe", mm_list, inc="pe", wait=[t_xwv, t_psS_rd])
            t_xstm_rd[j] = t_smm
            t_xw_rd[j] = t_smm
            t_sev = c.chain("dve", [lambda v, cc=cc, j=j: v.tensor_copy(
                sb_slocb[j][:], ps_S[:])],
                inc="dve", wait=[t_smm, t_slsp[j]])
            t_slsp[j] = c.dma("sp", sloc_dram[cc, :, :], sb_slocb[j][:],
                              wait=[t_sev], stream=f"slsp{j}")
            cur, nxt = cc % 2, (cc + 1) % 2
            t_tmp = c.chain("dve", [lambda v, cur=cur, cc=cc: v.tensor_tensor(
                sb_stmp.ap().rearrange("p (a b) -> p a b", a=16),
                sb_sA[cur].ap().rearrange("p (a b) -> p a b", a=16),
                sb_lam[:, cc * 16:(cc + 1) * 16].unsqueeze(2
                    ).broadcast_to([TILE, 16, 64]),
                OP.mult)], inc="dve", wait=[t_rec, t_lam])
            t_rec = c.chain("dve", [lambda v, nxt=nxt: v.scalar_tensor_tensor(
                sb_sA[nxt][:], ps_S[:], 0.0, sb_stmp[:], OP.add, OP.add)],
                inc="dve", wait=[t_tmp])
            t_psS_rd = t_rec
        if dev_stop == "mamba":
            c.dma("act", taps["t_sA"][:], sb_sA[0][:], wait=[t_rec],
                  stream="tap")
        t_stin = c.dma("sp", cc_st_in[:], sb_sA[0][:], wait=[t_rec],
                       stream="stio")
        t_agst = c.chain("pool", [lambda g: g.collective_compute(
            "AllGather", OP.bypass,
            replica_groups=[[0, 1], [2, 3], [4, 5], [6, 7]],
            ins=[cc_st_in[:]], outs=[cc_st_out[:]])], inc="cc",
            wait=[t_stin])
        t_stld = c.dma("sp", sb_stmp[:], cc_st_out[0:TILE, :], wait=[t_agst],
                       stream="stio")
        t_sB = c.chain("dve", [lambda v: v.tensor_scalar_mul(
            sb_sA[0][:], sb_stmp[:], sb_oddmask[:])], inc="dve",
            wait=[t_stld])
        c.barrier()
        if dev_stop == "mamba":
            c.dma("act", taps["t_sloc"][0:TILE, 0:1024],
                  sloc_dram[0, :, :], stream="tap")
        m4a.close()

        m4b = contextlib.ExitStack()
        sb_g0m = SBp(m4b, pfx + "g0m", [TILE, TILE])
        sb_srep = [SBp(m4b, pfx + f"srep{j}", [TILE, TILE]) for j in range(2)]
        sb_gpre = [SBp(m4b, pfx + f"gpre{j}", [TILE, TILE]) for j in range(2)]
        sb_gful = [SBp(m4b, pfx + f"gful{j}", [TILE, TILE], BF16)
                   for j in range(2)]
        sb_sBb = SBp(m4b, pfx + "sBb", [TILE, 1024], BF16)
        sb_yg = SBp(m4b, pfx + "yg", [TILE, DIN])
        sb_dxs = SBp(m4b, pfx + "dxs", [TILE, DIN], BF16)
        sb_ybf = SBp(m4b, pfx + "ybf", [TILE, DIN], BF16)
        sb_yfm = SBp(m4b, pfx + "yfm", [TILE, DIN], BF16)
        sb_zld = [SBp(m4b, pfx + "zld0", [TILE, DIN])] * 2
        sb_x2al = [SBp(m4b, pfx + "x2al0", [TILE, D])] * 2
        sb_x3 = [SBp(m4b, pfx + "x30", [TILE, D])] * 2
        sb_wout = SBp(m4b, pfx + "wout", [TILE, 16 * D], BF16)
        sb_rms = SBp(m4b, pfx + "rms", [TILE, 4])
        ps_A = PSp(m4b, pfx + "psA", [TILE, DIN])
        ps_bt = PSp(m4b, pfx + "psbt", [TILE, DIN], BF16)
        ps_misc = PSp(m4b, pfx + "psmisc", [TILE, 1024])

        tok_wo = c.dma("pool", sb_wout.ap().rearrange("p (k n) -> p k n", k=16),
                       W_outT.rearrange("(k p) n -> p k n", p=TILE),
                       stream="wout")
        t_zld_rd = [None, None]
        t_x2a_rd = [None, None]
        t_x3_wr = [None, None]
        t_srep_rd = [None, None]
        t_gpre_rd = [None, None]
        t_gful_rd = [None, None]
        t_psbt_ev = None
        t_g0m_rd = None
        t_recB = t_sB
        t_sBb_rd = None
        t_yg_done = None
        t_psA_free = None
        t_ybf_rd = None
        t_yfm_rd = None
        t_slB_rd = [None, None]
        t_dxs_rd = None
        for cc in range(16):
            j = cc % 2
            tok_z = c.dma("sp", sb_zld[j][:],
                          z_dram[cc * TILE:(cc + 1) * TILE, :],
                          wait=[t_zld_rd[0]], stream="zld")
            tok_xa = c.dma("sp", sb_x2al[j][:],
                           x2a_dram[cc * TILE:(cc + 1) * TILE, :],
                           wait=[t_x2a_rd[0]], stream="x2al")
            trs = [
                (lambda t, r=r, cc=cc: t.transpose(
                    ps_bt[:, r * TILE:(r + 1) * TILE],
                    sb_xsf[:, r * DIN + cc * TILE: r * DIN + (cc + 1) * TILE],
                    sb_ibf[:]))
                for r in range(16)
            ]
            t_tr = c.chain("pe", trs, inc="pe", wait=[t_psbt_ev])
            t_ev = c.chain("dve", [lambda v, j=j: v.tensor_copy(
                sb_xstm[j][:, 0:DIN], ps_bt[:, 0:DIN])], inc="dve",
                wait=[t_tr, t_xstm_rd[j]])
            t_psbt_ev = t_ev
            t_sbb = c.chain("dve", [lambda v, cur=cc % 2: v.tensor_copy(
                sb_sBb[:], sb_sA[cur][:])], inc="dve",
                wait=[t_recB, t_sBb_rd])
            mm_list = []
            for h2 in range(2):
                for nb in range(2):
                    lhs = ((lambda t, cc=cc: sb_c0b[:, cc * TILE:(cc + 1) * TILE])
                           if h2 == 0 else
                           (lambda t, cc=cc: sb_xsf[DSTATE:2 * DSTATE,
                                16 * DIN + cc * TILE: 16 * DIN + (cc + 1) * TILE]))
                    mm_list.append(
                        lambda t, h2=h2, nb=nb, lhs=lhs, cc=cc: t.matmul(
                            ps_A[:, h2 * 1024 + nb * 512:
                                 h2 * 1024 + (nb + 1) * 512],
                            lhs(t),
                            sb_sBb[h2 * 64:(h2 + 1) * 64,
                                   nb * 512:(nb + 1) * 512],
                            start=True, stop=True))
            t_yi = c.chain("pe", mm_list, inc="pe", wait=[t_sbb, t_psA_free])
            t_es = c.chain("dve", [lambda v, cc=cc: v.tensor_tensor(
                ps_A.ap().rearrange("p (h q) -> p h q", h=32),
                ps_A.ap().rearrange("p (h q) -> p h q", h=32),
                sb_escol[:, cc * 32:(cc + 1) * 32].unsqueeze(2
                    ).broadcast_to([TILE, 32, HEADDIM]),
                OP.mult)], inc="dve", wait=[t_yi, t_escol])
            tok_sl = c.dma("sp", sb_slocb[j][:], sloc_dram[cc, :, :],
                           wait=[t_slB_rd[j]], stream=f"slsp{j}")
            cur, nxt = cc % 2, (cc + 1) % 2
            t_tmpB = c.chain("dve", [lambda v, cur=cur, cc=cc: v.tensor_tensor(
                sb_stmp.ap().rearrange("p (a b) -> p a b", a=16),
                sb_sA[cur].ap().rearrange("p (a b) -> p a b", a=16),
                sb_lam[:, cc * 16:(cc + 1) * 16].unsqueeze(2
                    ).broadcast_to([TILE, 16, 64]),
                OP.mult)], inc="dve", wait=[t_yi])
            t_recB = c.chain("dve", [lambda v, nxt=nxt, j=j: v.tensor_tensor(
                sb_sA[nxt][:], sb_slocb[j][:],
                sb_stmp[:], OP.add)], inc="dve", wait=[t_tmpB, tok_sl])
            t_slB_rd[j] = t_recB
            t_sBb_rd = t_yi
            t_g0 = c.chain("pe", [lambda t, cc=cc: t.matmul(
                ps_misc[:, 0:TILE],
                sb_xsf[0:DSTATE, 16 * DIN + cc * TILE: 16 * DIN + (cc + 1) * TILE],
                sb_c0b[:, cc * TILE:(cc + 1) * TILE],
                start=True, stop=True)], inc="pe",
                wait=[t_g0m_rd, t_es])
            t_g0m = c.chain("dve", [lambda v: v.tensor_copy(
                sb_g0m[:], ps_misc[:, 0:TILE])],
                inc="dve", wait=[t_g0])
            t_intra_last = None
            for h in range(NHEADS):
                jh = h % 2
                t_srep = c.chain("dve", [lambda v, jh=jh, cc=cc, h=h: v.tensor_copy(
                    sb_srep[jh][:],
                    sb_scol[:, cc * 32 + h: cc * 32 + h + 1
                            ].broadcast_to([TILE, TILE]))],
                    inc="dve", wait=[t_srep_rd[jh]])
                t_dtr = c.chain("pe", [
                    lambda t, jh=jh: t.transpose(
                        ps_misc[:, 128 + jh * 384: 256 + jh * 384],
                        sb_srep[jh][:], sb_if32[:]),
                    lambda t, jh=jh: t.matmul(
                        ps_misc[:, 128 + jh * 384: 256 + jh * 384],
                        sb_if32[:], sb_maskneg[:],
                        start=False, stop=False, skip_group_check=True),
                ], inc="pe", wait=[t_srep, t_gpre_rd[jh]])
                t_srep_rd[jh] = t_dtr
                t_gp = c.chain("act", [lambda s, jh=jh, cc=cc, h=h: s.activation(
                    sb_gpre[jh][:], ps_misc[:, 128 + jh * 384: 256 + jh * 384],
                    AF.Exp, bias=sb_bcol[:, cc * 32 + h: cc * 32 + h + 1])],
                    inc="act", wait=[t_dtr, t_gful_rd[jh]])
                t_gpre_rd[jh] = t_gp
                t_gf = c.chain("pool", [lambda g, jh=jh: g.tensor_tensor(
                    sb_gful[jh][:], sb_gpre[jh][:], sb_g0m[:], OP.mult)],
                    inc="pool", wait=[t_gp, t_g0m, t_gful_rd[jh]])
                t_mi = c.chain("pe", [lambda t, jh=jh, j=j, h=h: t.matmul(
                    ps_A[:, h * HEADDIM:(h + 1) * HEADDIM],
                    sb_gful[jh][:],
                    sb_xstm[j][:, h * HEADDIM:(h + 1) * HEADDIM],
                    start=False, stop=False, skip_group_check=True)], inc="pe",
                    wait=[t_gf, t_es, t_ev])
                t_gful_rd[jh] = t_mi
                t_intra_last = t_mi
            t_g0m_rd = t_intra_last
            t_xstm_rd[j] = t_intra_last
            t_dxs = c.chain("dve", [lambda v, j=j: v.tensor_tensor(
                sb_dxs.ap().rearrange("p (h q) -> p h q", h=NHEADS),
                sb_xstm[j][:, 0:DIN].rearrange("p (h q) -> p h q", h=NHEADS),
                sb_dtm.ap().unsqueeze(2).broadcast_to([TILE, NHEADS, HEADDIM]),
                OP.mult)], inc="dve", wait=[t_dcol, t_ev, t_dxs_rd])
            t_yg = c.chain("dve", [
                lambda v: v.scalar_tensor_tensor(
                    sb_yg[:], ps_A[:], 0.0, sb_dxs[:], OP.add, OP.add),
                lambda v, j=j: v.tensor_mul(sb_yg[:], sb_yg[:], sb_zld[j][:]),
            ], inc="dve", wait=[t_intra_last, tok_z, t_yg_done, t_dxs])
            t_dxs_rd = t_yg
            t_psA_free = t_yg
            t_sq = c.chain("act", [lambda s: s.activation(
                sb_trash[:, 0:DIN], sb_yg[:], AF.Square,
                accum_out=sb_rms[:, 0:1])], inc="act", wait=[t_yg])
            t_rstd = c.serial_chain("act", [
                lambda s: s.activation(sb_rms[:, 1:2], sb_rms[:, 0:1],
                                       AF.Ln, scale=1.0 / DIN, bias=EPS),
                lambda s: s.activation(sb_rms[:, 2:3], sb_rms[:, 1:2],
                                       AF.Exp, scale=-0.5),
            ], wait=[t_sq])
            t_ybf = c.chain("dve", [lambda v: v.scalar_tensor_tensor(
                sb_ybf[:], sb_yg[:], sb_rms[:, 2:3], norm_w_rep_sb[:],
                OP.mult, OP.mult)], inc="dve", wait=[t_rstd, t_ybf_rd])
            t_yg_done = t_ybf
            if dev_stop == "mamba" and cc == 0:
                t_yg_done = c.dma("act", taps["t_yg0"][:], sb_yg[:],
                                  wait=[t_ybf], stream="tap")
                c.dma("act", taps["t_ybf0"][:], sb_ybf[:], wait=[t_ybf],
                      stream="tap")
            trs = [
                (lambda t, r=r: t.transpose(
                    ps_bt[:, r * TILE:(r + 1) * TILE],
                    sb_ybf[:, r * TILE:(r + 1) * TILE], sb_ibf[:]))
                for r in range(16)
            ]
            t_ytr = c.chain("pe", trs, inc="pe", wait=[t_ybf, t_psbt_ev])
            t_ybf_rd = t_ytr
            t_yfm = c.chain("dve", [
                lambda v: v.tensor_copy(sb_yfm[:], ps_bt[:, 0:DIN]),
            ], inc="dve", wait=[t_ytr, t_yfm_rd])
            t_psbt_ev = t_yfm
            mm_list = []
            for r in range(16):
                for nb in range(2):
                    mm_list.append(lambda t, r=r, nb=nb: t.matmul(
                        ps_A[:, nb * 512:(nb + 1) * 512],
                        sb_yfm[:, r * TILE:(r + 1) * TILE],
                        sb_wout[:, r * D + nb * 512: r * D + (nb + 1) * 512],
                        start=(r == 0), stop=(r == 15)))
            t_wo = c.chain("pe", mm_list, inc="pe",
                           wait=[t_yfm, tok_wo, t_yg])
            t_yfm_rd = t_wo
            if dev_stop == "mamba" and cc == 0:
                t_yfm_rd = c.dma("act", taps["t_yfm0"][:], sb_yfm[:],
                                 wait=[t_yfm], stream="tap")
            t_x3 = c.chain("dve", [lambda v, j=j: v.scalar_tensor_tensor(
                sb_x3[j][:], ps_A[:, 0:D], 0.0, sb_x2al[j][:],
                OP.add, OP.add)], inc="dve",
                wait=[t_wo, tok_xa, t_x3_wr[0]])
            t_psA_free = t_x3
            t_zld_rd[0] = t_yg
            t_x2a_rd[0] = t_x3
            t_x3_wr[0] = c.dma("sp", resid[(cc + 1) * TILE:(cc + 2) * TILE, :],
                               sb_x3[j][:], wait=[t_x3], stream="x3st")
        c.barrier()
        if dev_stop == "mamba":
            for ti in range(NTE):
                c.dma("act", taps["t_x3"][ti * TILE:(ti + 1) * TILE, :],
                      resid[ti * TILE:(ti + 1) * TILE, :], stream="tap")
        m4b.close()
        m4.close()
        mes.close()

    emit_ffn(1)

    if dev_stop == "ffn1":
        for ti in range(NTE):
            c.dma("sp", taps["t_x2"][ti * TILE:(ti + 1) * TILE, :],
                  resid[ti * TILE:(ti + 1) * TILE, :], stream="tap")
        c.barrier()
    elif dev_stop == "mamba":
        emit_mamba()
    else:
        emit_mamba()
        emit_ffn(2)

    _final = [(s, c.cnt[s]) for s in sorted(c.dma_sems) if c.cnt[s] > 0]
    c.on("sp", lambda e, _f=_final: [e.wait_ge(c.sem[s], v) for s, v in _f])

    with nc.Block() as block:
        @block.sync
        def _(sync):
            for f in c.seq["sp"]:
                f(sync)

        @block.tensor
        def _(t):
            for f in c.seq["pe"]:
                f(t)

        @block.scalar
        def _(s):
            for f in c.seq["act"]:
                f(s)

        @block.vector
        def _(v):
            for f in c.seq["dve"]:
                f(v)

        @block.gpsimd
        def _(g):
            for f in c.seq["pool"]:
                f(g)

    es.close()
    return nc


def prep_in_maps(inp):
    x = np.asarray(inp["L0_attn_x"], dtype=np.float32)
    phi_w = np.asarray(inp["L0_ffn_phi_w"], dtype=np.float32)
    phi_b = np.asarray(inp["L0_ffn_phi_b"], dtype=np.float32)
    expert_w = np.asarray(inp["L0_ffn_expert_w"], dtype=np.float32)
    expert_b = np.asarray(inp["L0_ffn_expert_b"], dtype=np.float32)
    W_in = np.asarray(inp["L0_attn_W_in"], dtype=np.float32)
    W_out = np.asarray(inp["L0_attn_W_out"], dtype=np.float32)
    conv_w = np.asarray(inp["L0_attn_conv_w"], dtype=np.float32)
    conv_b = np.asarray(inp["L0_attn_conv_b"], dtype=np.float32)
    dt_bias = np.asarray(inp["L0_attn_dt_bias"], dtype=np.float32)
    A_log = np.asarray(inp["L0_attn_A_log"], dtype=np.float32)
    Dd = np.asarray(inp["L0_attn_D"], dtype=np.float32)
    norm_w = np.asarray(inp["L0_attn_norm_w"], dtype=np.float32)
    ln_w = np.asarray(inp["L0_attn_ln_w"], dtype=np.float32)
    ln_b = np.asarray(inp["L0_attn_ln_b"], dtype=np.float32)

    common = {
        "phi_wT": np.ascontiguousarray(phi_w.T),
        "phi_b": phi_b[None, :],
        "W_inT": np.ascontiguousarray(W_in.T),
        "W_outT": np.ascontiguousarray(W_out.T),
        "conv_w": np.ascontiguousarray(
            conv_w.reshape(17, TILE, 4).transpose(1, 0, 2).reshape(TILE, 68)),
        "conv_b": np.ascontiguousarray(conv_b.reshape(17, TILE).T),
        "dt_bias": dt_bias[:, None],
        "A_log": A_log[:, None],
        "D_row": Dd[None, :],
        "norm_w_rep": np.ascontiguousarray(np.broadcast_to(norm_w, (TILE, DIN))),
        "ln_w_cols": np.ascontiguousarray(ln_w.reshape(8, TILE).T),
        "ln_b_cols": np.ascontiguousarray(ln_b.reshape(8, TILE).T),
        "i_bf": np.eye(TILE, dtype=np.float32).astype(ml_dtypes.bfloat16),
        "i_f32": np.eye(TILE, dtype=np.float32),
        "maskT": np.triu(np.ones((TILE, TILE), dtype=np.float32)),
        "maskneg": np.where(np.triu(np.ones((TILE, TILE))) > 0, 0.0,
                            -30000.0).astype(np.float32),
        "crows_f": np.concatenate([
            np.ones(TILE, np.float32),
            np.concatenate([np.ones(64, np.float32), np.zeros(64, np.float32)]),
            np.concatenate([np.zeros(64, np.float32), np.ones(64, np.float32)]),
            np.zeros(TILE, np.float32)])[None, :],
        "crows_b": np.ones((1, TILE), np.float32).astype(ml_dtypes.bfloat16),
    }
    maps = []
    for cid in range(NCORES):
        b, h = cid // 2, cid % 2
        real = x[b, h * TREAL:(h + 1) * TREAL]
        prefix = (x[b, TREAL - TILE:TREAL] if h == 1
                  else np.zeros((TILE, D), np.float32))
        m = dict(common)
        m["x_slice"] = np.ascontiguousarray(np.concatenate([prefix, real], 0))
        m["expw"] = np.ascontiguousarray(expert_w[cid * EXP_LOC:(cid + 1) * EXP_LOC])
        m["expb_rep"] = np.ascontiguousarray(
            np.repeat(expert_b[cid * EXP_LOC:(cid + 1) * EXP_LOC], PSLOT, axis=0))
        m["pmask"] = np.full((TILE, 4), float(h), np.float32)
        m["oddmask"] = np.full((TILE, 1), float(h), np.float32)
        maps.append(m)
    return maps


def kernel(**inputs):
    maps = prep_in_maps(inputs)
    nc = build_program(dev_stop=DEV_STOP)
    res = run_bass_kernel_spmd(nc, maps, core_ids=list(range(NCORES)))
    out = np.empty((4, 4096, D), dtype=np.float32)
    key = {"ffn1": "t_x2", "mamba": "t_x3"}.get(DEV_STOP, "y_out")
    for cid in range(NCORES):
        b, h = cid // 2, cid % 2
        r = res.results[cid][key]
        if key in ("t_x2", "t_x3"):
            r = r[TILE:]
        out[b, h * TREAL:(h + 1) * TREAL] = r
    return out


# version 02
# speedup vs baseline: 1.1289x; 1.1289x over previous
import contextlib

import numpy as np
import ml_dtypes

import concourse.bass as bass
import concourse.mybir as mybir
from concourse.bass_utils import run_bass_kernel_spmd

F32 = mybir.dt.float32
BF16 = mybir.dt.bfloat16
AF = mybir.ActivationFunctionType
OP = mybir.AluOpType
AX = mybir.AxisListType

NCORES = 8
D = 1024
TILE = 128
NT = 16
NTE = 17
TREAL = NT * TILE
TTOT = NTE * TILE
DIN = 2048
DSTATE = 64
NHEADS = 32
HEADDIM = 64
CONVD = DIN + 2 * DSTATE
DPROJ = 2 * DIN + 2 * DSTATE + NHEADS
NEXP = 64
PSLOT = 32
EXP_LOC = NEXP // NCORES
QCH = 128
NCH = TREAL // QCH
EPS = 1e-5

ENGINES = ("sp", "pe", "act", "dve", "pool")

DEV_STOP = None


class Ctx:

    def __init__(self, nc, sem_stack):
        self.nc = nc
        self.seq = {e: [] for e in ENGINES}
        self.cnt = {}
        self.sem = {}
        self.nbar = 0
        self.dma_sems = set()
        self._sem_stack = sem_stack

    def new_sem(self, name, handle):
        self.sem[name] = handle
        self.cnt[name] = 0

    def on(self, eng, fn):
        self.seq[eng].append(fn)

    def chain(self, eng, emits, inc=None, inc_n=1, wait=()):
        if not isinstance(emits, (list, tuple)):
            emits = [emits]
        sems = self.sem
        tok = None
        if inc is not None:
            self.cnt[inc] += inc_n
            tok = (inc, self.cnt[inc])
        waits = [w for w in wait if w is not None]

        def f(eng_obj, _emits=list(emits), _w=waits, _inc=inc, _n=inc_n):
            for s, v in _w:
                eng_obj.wait_ge(sems[s], v)
            inst = None
            for em in _emits:
                inst = em(eng_obj)
            if _inc is not None:
                inst.then_inc(sems[_inc], _n)

        self.on(eng, f)
        return tok

    def serial_chain(self, eng, emits, inc=None, wait=()):
        tok = None
        for i, em in enumerate(emits):
            tok = self.chain(eng, [em], inc=inc or eng,
                             wait=wait if i == 0 else [tok])
        return tok

    def dma(self, eng, out_ap, in_ap, wait=(), stream=None):
        s = "dma_" + (stream or (eng + "_misc"))
        if s not in self.sem:
            self.new_sem(s, self._sem_stack.enter_context(self.nc.semaphore(s)))
            self.dma_sems.add(s)
        return self.chain(
            eng, lambda e, o=out_ap, i=in_ap: e.dma_start(out=o, in_=i),
            inc=s, inc_n=16, wait=wait)

    def wait(self, eng, *toks):
        sems = self.sem
        toks = [t for t in toks if t is not None]
        if toks:
            self.on(eng, lambda e, _t=list(toks): [e.wait_ge(sems[s], v) for s, v in _t])

    def barrier(self):
        self.nbar += 1
        sems = self.sem
        n = len(ENGINES)
        dma_waits = [(s, self.cnt[s]) for s in sorted(self.dma_sems)
                     if self.cnt[s] > 0]
        for eng in ENGINES:
            waits = list(dma_waits) if eng == "sp" else []
            if eng == "pool" and self.cnt.get("cc", 0) > 0:
                waits.append(("cc", self.cnt["cc"]))

            def f(e, _w=waits, _k=self.nbar, _n=n):
                for s, v in _w:
                    e.wait_ge(sems[s], v)
                e.sem_inc(sems["bar"], 1)
                e.wait_ge(sems["bar"], _k * _n)

            self.on(eng, f)
        self.cnt["bar"] += n


def build_program(dev_stop=None):
    nc = bass.Bass("TRN2", target_bir_lowering=False, debug=False,
                   detect_race_conditions=False)
    es = contextlib.ExitStack()

    def din(name, shape, dtype=F32):
        return nc.dram_tensor(name, list(shape), dtype, kind="ExternalInput").ap()

    x_slice = din("x_slice", [TTOT, D])
    phi_wT = din("phi_wT", [D, DIN])
    phi_b = din("phi_b", [1, DIN])
    W_inT = din("W_inT", [D, DPROJ])
    W_outT = din("W_outT", [DIN, D])
    expw = din("expw", [EXP_LOC, D, D])
    expb_rep = din("expb_rep", [EXP_LOC * PSLOT, D])
    conv_w = din("conv_w", [TILE, 17 * 4])
    conv_b = din("conv_b", [TILE, 17])
    dt_bias = din("dt_bias", [NHEADS, 1])
    A_log = din("A_log", [NHEADS, 1])
    D_row = din("D_row", [1, NHEADS])
    norm_w_rep = din("norm_w_rep", [TILE, DIN])
    ln_w_cols = din("ln_w_cols", [TILE, D // TILE])
    ln_b_cols = din("ln_b_cols", [TILE, D // TILE])
    i_bf = din("i_bf", [TILE, TILE], BF16)
    i_f32 = din("i_f32", [TILE, TILE])
    maskT_in = din("maskT", [TILE, TILE])
    maskneg_in = din("maskneg", [TILE, 4 * TILE])
    crows_f = din("crows_f", [1, 4 * TILE])
    crows_b = din("crows_b", [1, TILE], BF16)
    pmask = din("pmask", [TILE, 4])
    oddmask = din("oddmask", [TILE, 1])

    y_out = nc.dram_tensor("y_out", [TREAL, D], F32, kind="ExternalOutput").ap()

    def dint(name, shape, dtype=F32):
        return nc.dram_tensor(name, list(shape), dtype, kind="Internal").ap()

    resid = dint("resid", [TTOT, D])
    x2a_dram = dint("x2a_dram", [TREAL, D])
    eprime = dint("eprime", [16, TILE, TTOT], BF16)
    xbcpre = dint("xbcpre", [NTE, TILE, TREAL + 3], BF16)
    z_dram = dint("z_dram", [TREAL, DIN], BF16)
    cc_xt_in = dint("cc_xt_in", [DIN, D])
    cc_xt_out = dint("cc_xt_out", [EXP_LOC * PSLOT, D])
    cc_ag_in = dint("cc_ag_in", [EXP_LOC * PSLOT, D], BF16)
    cc_ag_out = dint("cc_ag_out", [NEXP * PSLOT, D], BF16)
    sloc_dram = dint("sloc_dram", [16, TILE, 1024], BF16)
    cc_st_in = dint("cc_st_in", [TILE, 16 * DSTATE], F32)
    cc_st_out = dint("cc_st_out", [2 * TILE, 16 * DSTATE], F32)

    taps = {}
    if dev_stop == "mamba":
        def tout(name, shape, dtype=F32):
            taps[name] = nc.dram_tensor(name, list(shape), dtype,
                                        kind="ExternalOutput").ap()
        tout("t_x3", [TTOT, D])
        tout("t_ufm", [TILE, 8 * TTOT], BF16)
        tout("t_xbc", [NTE, TILE, TREAL + 3], BF16)
        tout("t_z", [TREAL, DIN], BF16)
        tout("t_sg", [NHEADS, TREAL])
        tout("t_vecs", [TILE, 512 * 4 + 256 + 32])
        tout("t_xsf", [TILE, 17 * DIN], BF16)
        tout("t_sloc", [TILE, 16 * 1024], BF16)
        tout("t_x2a", [TREAL, D])
        tout("t_sA", [TILE, 1024])
        tout("t_yg0", [TILE, DIN])
        tout("t_ybf0", [TILE, DIN], BF16)
        tout("t_yfm0", [TILE, DIN], BF16)
    if dev_stop == "ffn1":
        def tout(name, shape, dtype=F32):
            taps[name] = nc.dram_tensor(name, list(shape), dtype,
                                        kind="ExternalOutput").ap()
        tout("t_x2", [TTOT, D])
        tout("t_xln", [TILE, NTE * D], BF16)
        tout("t_dall", [TILE, NTE * DIN], BF16)
        tout("t_e", [TILE, DIN], BF16)
        tout("t_sums", [TILE, 192])
        tout("t_rcr", [TILE, TILE])
        tout("t_eprime", [16, TILE, TTOT], BF16)
        tout("t_xtin", [DIN, D])
        tout("t_xtout", [EXP_LOC * PSLOT, D])
        tout("t_agout", [NEXP * PSLOT, D], BF16)
        tout("t_xt01", [2 * TILE, D])
        tout("t_st01", [2 * TILE, 32])

    c = Ctx(nc, es)
    for s in ["pe", "act", "dve", "pool", "cc", "bar"]:
        c.new_sem(s, es.enter_context(nc.semaphore(s)))

    def SBp(stack, name, shape, dtype=F32):
        return stack.enter_context(nc.sbuf_tensor(name, list(shape), dtype))

    def PSp(stack, name, shape, dtype=F32):
        return stack.enter_context(nc.psum_tensor(name, list(shape), dtype))

    sb_ibf = SBp(es, "c_ibf", [TILE, TILE], BF16)
    sb_if32 = SBp(es, "c_if32", [TILE, TILE])
    sb_maskT = SBp(es, "c_maskT", [TILE, TILE])
    sb_maskneg = SBp(es, "c_maskneg", [TILE, 4 * TILE])
    sb_crf = SBp(es, "c_crf", [1, 4 * TILE])
    sb_onescol = SBp(es, "c_onescol", [TILE, 1])
    sb_crb = SBp(es, "c_crb", [1, TILE], BF16)
    sb_phib = SBp(es, "c_phib", [1, DIN], BF16)
    sb_pmask = SBp(es, "c_pmask", [TILE, 4])
    sb_oddmask = SBp(es, "c_oddmask", [TILE, 1])
    ln_w_cols_sb = SBp(es, "c_lnw", [TILE, D // TILE])
    ln_b_cols_sb = SBp(es, "c_lnb", [TILE, D // TILE])
    norm_w_rep_sb = SBp(es, "c_nwrep", [TILE, DIN], BF16)
    sb_dp = SBp(es, "c_dp", [1, NHEADS])

    c.chain("pool", [lambda g: g.memset(sb_onescol[:], 1.0)])

    _flt_vals = [EPS, -0.5, -1.0, 0.5, 1.0 / DIN]
    sb_flt = SBp(es, "c_flt", [TILE, len(_flt_vals)])
    for _i, _v in enumerate(_flt_vals):
        _ap = sb_flt[:, _i:_i + 1]
        c.chain("pool", [lambda g, a=_ap, v=_v: g.memset(a, v)])
        nc.const_aps.register(F32, _v, _ap)
    for dst, src, q in [(sb_ibf, i_bf, "sp"), (sb_if32, i_f32, "sp"),
                        (sb_maskT, maskT_in, "sp"),
                        (sb_maskneg, maskneg_in, "sp"),
                        (sb_crf, crows_f, "sp"),
                        (sb_crb, crows_b, "sp"), (sb_pmask, pmask, "sp"),
                        (sb_oddmask, oddmask, "sp"),
                        (ln_w_cols_sb, ln_w_cols, "sp"),
                        (ln_b_cols_sb, ln_b_cols, "sp"),
                        (sb_dp, D_row, "sp")]:
        c.dma(q, dst[:], src[:], stream="const")
    c.dma("pool", sb_phib[:], phi_b[:], stream="constp")
    c.dma("pool", norm_w_rep_sb[:], norm_w_rep[:], stream="constp")
    c.barrier()

    ST_W = 32
    sb_trash = SBp(es, "c_trash", [TILE, 2048])

    def ln_chain(st, src_ap, width, waits=()):
        s1, sq, msq = st[:, 20:21], st[:, 21:22], st[:, 22:23]
        mu, var = st[:, 24:25], st[:, 25:26]
        lnt, rstd = st[:, 26:27], st[:, 27:28]
        c.chain("dve", [
            lambda v: v.scalar_tensor_tensor(
                sb_trash[:, 0:width], src_ap[:, 0:width], 1.0,
                src_ap[:, 0:width], OP.mult, OP.mult, accum_out=sq),
        ], inc="dve", wait=waits)
        t_acc = c.chain("dve", [
            lambda v: v.reduce_sum(s1, src_ap[:, 0:width], AX.X),
        ], inc="dve")
        t_stats = c.serial_chain("dve", [
            lambda v: v.tensor_scalar_mul(mu, s1, 1.0 / width),
            lambda v: v.tensor_scalar_mul(sq, sq, 1.0 / width),
            lambda v: v.tensor_scalar_mul(msq, mu, mu),
            lambda v: v.tensor_sub(var, sq, msq),
        ], wait=[t_acc])
        t_rstd = c.chain("act", [
            lambda s: s.activation(lnt, var, AF.Ln, bias=EPS),
            lambda s: s.activation(rstd, lnt, AF.Exp, scale=-0.5),
        ], inc="act", wait=[t_stats])
        return mu, rstd, t_rstd

    def emit_ffn(pass_id):
        first = pass_id == 1
        dev1 = (dev_stop == "ffn1" and first)
        tiles = list(range(NTE)) if first else list(range(1, NTE))
        x_src = x_slice if first else resid

        pa = contextlib.ExitStack()
        pfx = f"f{pass_id}"
        sb_xln = SBp(pa, pfx + "xln", [TILE, len(tiles) * D], BF16)
        sb_dall = SBp(pa, pfx + "dall", [TILE, len(tiles) * DIN], BF16)

        paa = contextlib.ExitStack()
        sb_phi = SBp(paa, pfx + "phi", [TILE, 8 * DIN], BF16)
        sb_xt = [SBp(paa, pfx + f"xt{j}", [TILE, D]) for j in range(2)]
        sb_xlf = [SBp(paa, pfx + f"xlf{j}", [TILE, 8 * TILE], BF16) for j in range(2)]
        sb_e = [SBp(paa, pfx + f"e{j}", [TILE, DIN], BF16) for j in range(2)]
        sb_eps = [SBp(paa, pfx + f"eps{j}", [TILE, 16 * TILE], BF16) for j in range(2)]
        sb_st = [SBp(paa, pfx + f"st{j}", [TILE, ST_W]) for j in range(2)]
        sb_sums = [SBp(paa, pfx + f"sums{j}", [TILE, 192]) for j in range(2)]
        sb_rcr = [SBp(paa, pfx + f"rcr{j}", [TILE, TILE]) for j in range(2)]
        ps_lg = PSp(paa, pfx + "pslg", [TILE, DIN])
        ps_tr = PSp(paa, pfx + "pstr", [TILE, 8 * TILE], BF16)
        ps_et = PSp(paa, pfx + "pset", [TILE, 16 * TILE], BF16)
        ps_rc = PSp(paa, pfx + "psrc", [TILE, TILE])

        tok_phi = c.dma("pool", sb_phi.ap().rearrange("p (k n) -> p k n", k=8),
                        phi_wT.rearrange("(k p) n -> p k n", p=TILE),
                        stream="phi")
        c.wait("pe", tok_phi)

        t_xt_rd = [None, None]
        t_xlf_rd = [None, None]
        t_e_rd = [None, None]
        t_eps_sp = [None, None]
        t_rstd_prev = [None, None]
        t_rct_prev = [None, None]
        t_pstr_ev = None
        t_pslg_exp = [None, None]
        t_psrc_cp = None
        t_pset_ev = None

        for ii, ti in enumerate(tiles):
            j = ii % 2
            xt, xlf, E, epst = sb_xt[j], sb_xlf[j], sb_e[j], sb_eps[j]
            st, sums, rcr = sb_st[j], sb_sums[j], sb_rcr[j]
            gmaxn = st[:, 28:29]

            tok_x = c.dma("sp", xt[:], x_src[ti * TILE:(ti + 1) * TILE, :],
                          wait=[t_xt_rd[j]], stream=f"x{j}")
            mu, rstd, t_rstd = ln_chain(st, xt, D,
                                        waits=[tok_x, t_rstd_prev[j]])
            t_rstd_prev[j] = t_rstd
            xln_t = sb_xln[:, ii * D:(ii + 1) * D]
            t_ln = c.chain("dve", [
                lambda v, a=xt, o=xln_t, m=mu, r=rstd: v.tensor_scalar(
                    o, a[:], m, r, OP.subtract, OP.mult),
            ], inc="dve", wait=[t_rstd])
            t_xt_rd[j] = t_ln
            if dev1 and ti in (0, 1):
                t1 = c.dma("act", taps["t_xt01"][ti * TILE:(ti + 1) * TILE, :],
                           xt[:], wait=[t_ln], stream="tap")
                t2 = c.dma("act", taps["t_st01"][ti * TILE:(ti + 1) * TILE, :],
                           st[:], wait=[t_ln], stream="tap")
                t_xt_rd[j] = t2
                t_rstd_prev[j] = t2

            t_tr = c.chain("pe", [
                (lambda t, k=k, o=ps_tr, i_=xln_t: t.transpose(
                    o[:, k * TILE:(k + 1) * TILE],
                    i_[:, k * TILE:(k + 1) * TILE], sb_ibf[:]))
                for k in range(8)
            ], inc="pe", wait=[t_ln, t_pstr_ev])
            t_xlf_cp = c.chain("dve", [
                lambda v, o=xlf: v.tensor_copy(o[:], ps_tr[:]),
            ], inc="dve", wait=[t_tr, t_xlf_rd[j]])
            t_pstr_ev = t_xlf_cp

            t_lgh = [None, None]
            for half in range(2):
                mm_list = []
                for k in range(8):
                    for nb in (2 * half, 2 * half + 1):
                        mm_list.append(lambda t, k=k, nb=nb, _x=xlf: t.matmul(
                            ps_lg[:, nb * 512:(nb + 1) * 512],
                            _x[:, k * TILE:(k + 1) * TILE],
                            sb_phi[:, k * DIN + nb * 512: k * DIN + (nb + 1) * 512],
                            start=(k == 0), stop=False))
                for nb in (2 * half, 2 * half + 1):
                    mm_list.append(lambda t, nb=nb: t.matmul(
                        ps_lg[:, nb * 512:(nb + 1) * 512], sb_crb[0:1, :],
                        sb_phib[0:1, nb * 512:(nb + 1) * 512],
                        start=False, stop=True))
                t_lgh[half] = c.chain("pe", mm_list, inc="pe",
                                      wait=[t_xlf_cp, t_pslg_exp[half]])
                t_pslg_exp[half] = c.chain("act", [
                    lambda s, o=E, half=half: s.activation(
                        o[:, half * 1024:(half + 1) * 1024],
                        ps_lg[:, half * 1024:(half + 1) * 1024], AF.Exp),
                ], inc="act", wait=[t_lgh[half], t_e_rd[j]])
            t_xlf_rd[j] = t_lgh[1]
            t_exp = t_pslg_exp[1]
            c.wait("dve", t_pslg_exp[0])

            sd, sc = sums[:, 0:64], sums[:, 64:96]
            rd, rc = sums[:, 96:160], sums[:, 160:192]
            dall_t = sb_dall[:, ii * DIN:(ii + 1) * DIN]
            t_red = c.chain("dve", [
                lambda v, a=E, o=sd: v.reduce_sum(
                    o.rearrange("p (n q) -> p n q", q=1),
                    a.ap().rearrange("p (n q) -> p n q", n=64), AX.X),
                lambda v, a=E, o=sc: v.reduce_sum(
                    o.rearrange("p (n q) -> p n q", q=1),
                    a.ap().rearrange("p (n q) -> p n q", n=64).transpose([0, 2, 1]),
                    AX.X),
            ], inc="dve", wait=[t_exp])
            t_rc = c.chain("dve", [
                lambda v, o=rd, i_=sd: v.reciprocal(o, i_),
                lambda v, o=rc, i_=sc: v.reciprocal(o, i_),
            ], inc="dve", wait=[t_red])
            c.chain("dve", [
                lambda v, a=E, o=dall_t, r=rd: v.tensor_tensor(
                    o.rearrange("p (a b) -> p a b", a=64),
                    a.ap().rearrange("p (a b) -> p a b", a=64),
                    r.unsqueeze(2).broadcast_to([TILE, 64, 32]),
                    OP.mult),
            ], wait=[t_rc])
            t_rc4 = c.chain("dve", [
                lambda v, o=rcr, r=rc: v.tensor_copy(
                    o.ap().rearrange("p (a b) -> p a b", a=4),
                    r.unsqueeze(1).broadcast_to([TILE, 4, 32])),
            ], inc="dve", wait=[t_rct_prev[j], t_rc])
            t_rct = c.chain("pe", [
                lambda t, i_=rcr: t.transpose(ps_rc[:], i_[:], sb_if32[:]),
            ], inc="pe", wait=[t_rc4, t_psrc_cp])
            t_rct_prev[j] = t_rct
            t_rcr_cp = c.chain("dve", [
                lambda v, o=rcr: v.tensor_copy(o[:], ps_rc[:]),
            ], inc="dve", wait=[t_rct])
            t_psrc_cp = t_rcr_cp

            t_ett = c.chain("pe", [
                (lambda t, m=m, a=E: t.transpose(
                    ps_et[:, m * TILE:(m + 1) * TILE],
                    a[:, m * TILE:(m + 1) * TILE], sb_ibf[:]))
                for m in range(16)
            ], inc="pe", wait=[t_exp, t_pset_ev])
            t_e_rd[j] = t_ett
            t_epsv = c.chain("dve", [
                lambda v, o=epst, r=rcr: v.tensor_tensor(
                    o.ap().rearrange("p (a b) -> p a b", a=16),
                    ps_et.ap().rearrange("p (a b) -> p a b", a=16),
                    r.ap().unsqueeze(1).broadcast_to([TILE, 16, TILE]),
                    OP.mult),
            ], inc="dve", wait=[t_ett, t_eps_sp[j], t_rcr_cp])
            t_pset_ev = t_epsv
            t_eps_sp[j] = c.dma(
                "sp", eprime[:, :, ti * TILE:(ti + 1) * TILE].transpose([1, 0, 2]),
                epst.ap().rearrange("p (m t) -> p m t", m=16), wait=[t_epsv],
                stream=f"epsp{j}")
            if dev1 and ti == 3:
                c.dma("act", taps["t_e"][:], E[:], wait=[t_exp], stream="tap")
                c.dma("act", taps["t_sums"][:], sums[:], wait=[t_rc], stream="tap")
                c.dma("act", taps["t_rcr"][:], rcr[:], wait=[t_rcr_cp], stream="tap")

        c.barrier()
        if dev1:
            c.dma("act", taps["t_xln"][:], sb_xln[:], stream="tap")
            c.dma("act", taps["t_dall"][:], sb_dall[:], stream="tap")
            c.dma("act", taps["t_eprime"][:], eprime[:], stream="tap")
        paa.close()

        pb = contextlib.ExitStack()
        ps_x2 = [PSp(pb, pfx + f"psxt{j}", [TILE, D]) for j in range(2)]
        sb_xp = [SBp(pb, pfx + f"xp{j}", [TILE, D]) for j in range(2)]
        real_ii = [ii for ii, t0 in enumerate(tiles) if t0 >= 1]
        t_x2ev = [None, None]
        t_xp_wr = [None, None]
        for m in range(16):
            j = m % 2
            mm_list = []
            for kk, ii in enumerate(real_ii):
                for nb in range(2):
                    mm_list.append(lambda t, m=m, ii=ii, nb=nb, j=j, kk=kk: t.matmul(
                        ps_x2[j][:, nb * 512:(nb + 1) * 512],
                        sb_dall[:, ii * DIN + m * TILE: ii * DIN + (m + 1) * TILE],
                        sb_xln[:, ii * D + nb * 512: ii * D + (nb + 1) * 512],
                        start=(kk == 0), stop=(kk == len(real_ii) - 1)))
            t_mm = c.chain("pe", mm_list, inc="pe", wait=[t_x2ev[j]])
            t_ev = c.chain("dve", [
                lambda v, j=j: v.tensor_copy(sb_xp[j][:], ps_x2[j][:]),
            ], inc="dve", wait=[t_mm, t_xp_wr[j]])
            t_x2ev[j] = t_ev
            t_xp_wr[j] = c.dma("sp", cc_xt_in[m * TILE:(m + 1) * TILE, :],
                               sb_xp[j][:], wait=[t_ev], stream=f"xpst{j}")
        c.barrier()
        pb.close()
        pa.close()
        if dev1:
            c.dma("act", taps["t_xtin"][:], cc_xt_in[:], stream="tap")

        t_rs = c.chain("pool", [lambda g: g.collective_compute(
            "ReduceScatter", OP.add,
            replica_groups=[[0, 1, 2, 3, 4, 5, 6, 7]],
            ins=[cc_xt_in[:]], outs=[cc_xt_out[:]])], inc="cc")
        if dev1:
            c.dma("act", taps["t_xtout"][:], cc_xt_out[:], wait=[t_rs], stream="tap")

        pc = contextlib.ExitStack()
        sb_xts = SBp(pc, pfx + "xts", [TILE, 2 * D])
        sb_xtb = SBp(pc, pfx + "xtb", [TILE, 2 * D], BF16)
        sb_xtf = SBp(pc, pfx + "xtf", [TILE, 8 * 256], BF16)
        sb_ew = [SBp(pc, pfx + f"ew{j}", [TILE, 8 * D], BF16) for j in range(2)]
        sb_ytb = SBp(pc, pfx + "ytb", [TILE, 2 * D])
        sb_ytbb = SBp(pc, pfx + "ytbb", [TILE, 2 * D], BF16)
        sb_eb = SBp(pc, pfx + "eb", [TILE, 2 * D])
        sb_st4 = SBp(pc, pfx + "st4", [TILE, ST_W])
        ps_e = PSp(pc, pfx + "pse", [TILE, D])
        ps_xtt = PSp(pc, pfx + "psxtt", [TILE, 8 * 256], BF16)

        tok_xt = c.dma("sp", sb_xts.ap().rearrange("p (a d) -> p a d", a=2),
                       cc_xt_out.rearrange("(a p) d -> p a d", p=TILE),
                       wait=[t_rs], stream="xtrs")
        tok_eb = c.dma("sp", sb_eb.ap().rearrange("p (a d) -> p a d", a=2),
                       expb_rep.rearrange("(a p) d -> p a d", p=TILE),
                       stream="eb")
        tok_ew = [None] * EXP_LOC
        tok_ew[0] = c.dma("pool",
                          sb_ew[0].ap().rearrange("p (k h) -> p k h", k=8),
                          expw[0].rearrange("(k p) h -> p k h", p=TILE),
                          stream="ew0")

        t_clip = c.chain("dve", [lambda v: v.tensor_scalar(
            sb_xts[:], sb_xts[:], -33000.0, 65000.0, OP.max, OP.min)],
            inc="dve", wait=[tok_xt])
        t_tanh = None
        for a in range(2):
            mu4, rstd4, t_r = ln_chain(sb_st4, sb_xts[:, a * D:(a + 1) * D], D,
                                       waits=[t_clip, t_tanh])
            t_ap = c.chain("dve", [lambda v, a=a, m=mu4, r=rstd4: v.tensor_scalar(
                sb_xts[:, a * D:(a + 1) * D], sb_xts[:, a * D:(a + 1) * D],
                m, r, OP.subtract, OP.mult)], inc="dve", wait=[t_r])
            t_tanh = c.chain("act", [lambda s, a=a: s.activation(
                sb_xtb[:, a * D:(a + 1) * D], sb_xts[:, a * D:(a + 1) * D],
                AF.Tanh)], inc="act", wait=[t_ap])

        t_xtt = c.chain("pe", [
            (lambda t, a=a, k=k: t.transpose(
                ps_xtt[:, k * 256 + a * TILE: k * 256 + (a + 1) * TILE],
                sb_xtb[:, a * D + k * TILE: a * D + (k + 1) * TILE],
                sb_ibf[:]))
            for a in range(2) for k in range(8)
        ], inc="pe", wait=[t_tanh])
        t_xtf = c.chain("dve", [lambda v: v.tensor_copy(sb_xtf[:], ps_xtt[:])],
                        inc="dve", wait=[t_xtt])

        t_eg_prev = None
        t_mm_exp = [None] * EXP_LOC
        for g4 in range(2):
            for nl in range(4):
                n = g4 * 4 + nl
                if n + 1 < EXP_LOC:
                    tok_ew[n + 1] = c.dma(
                        "pool",
                        sb_ew[(n + 1) % 2].ap().rearrange("p (k h) -> p k h", k=8),
                        expw[n + 1].rearrange("(k p) h -> p k h", p=TILE),
                        wait=[t_mm_exp[n - 1]] if n >= 1 else [],
                        stream=f"ew{(n + 1) % 2}")
                mm_list = []
                for k in range(8):
                    for nb in range(2):
                        mm_list.append(lambda t, n=n, nl=nl, k=k, nb=nb: t.matmul(
                            ps_e[nl * 32:(nl + 1) * 32, nb * 512:(nb + 1) * 512],
                            sb_xtf[:, k * 256 + n * 32: k * 256 + n * 32 + 32],
                            sb_ew[n % 2][:, k * D + nb * 512: k * D + (nb + 1) * 512],
                            start=(k == 0), stop=(k == 7),
                            tile_position=(0, nl * 32)))
                t_mm_exp[n] = c.chain("pe", mm_list, inc="pe",
                                      wait=[t_xtf, tok_ew[n], t_eg_prev])
            t_eg_prev = c.chain("dve", [lambda v, g4=g4: v.tensor_copy(
                sb_ytb[:, g4 * D:(g4 + 1) * D], ps_e[:])], inc="dve",
                wait=[t_mm_exp[g4 * 4 + 3]])
        t_bias = c.chain("dve", [lambda v: v.tensor_tensor(
            sb_ytbb[:], sb_ytb[:], sb_eb[:], OP.add)], inc="dve",
            wait=[tok_eb, t_eg_prev])
        c.dma("sp", cc_ag_in.rearrange("(a p) d -> p a d", p=TILE),
              sb_ytbb.ap().rearrange("p (a d) -> p a d", a=2), wait=[t_bias],
              stream="agst")
        c.barrier()
        pc.close()

        t_ag = c.chain("pool", [lambda g: g.collective_compute(
            "AllGather", OP.bypass,
            replica_groups=[[0, 1, 2, 3, 4, 5, 6, 7]],
            ins=[cc_ag_in[:]], outs=[cc_ag_out[:]])], inc="cc")
        if dev1:
            c.dma("act", taps["t_agout"][:], cc_ag_out[:], wait=[t_ag], stream="tap")

        pd = contextlib.ExitStack()
        sb_yt = SBp(pd, pfx + "yt", [TILE, 16 * D], BF16)
        sb_ep = [SBp(pd, pfx + f"ep{j}", [TILE, 16 * TILE], BF16) for j in range(2)]
        sb_x = [SBp(pd, pfx + f"x{j}", [TILE, D]) for j in range(2)]
        sb_x2 = [SBp(pd, pfx + f"x2{j}", [TILE, D]) for j in range(2)]
        ps_y = [PSp(pd, pfx + f"psy{j}", [TILE, D]) for j in range(2)]

        tok_yt = c.dma("sp", sb_yt.ap().rearrange("p (m h) -> p m h", m=16),
                       cc_ag_out.rearrange("(m p) h -> p m h", p=TILE),
                       wait=[t_ag], stream="yt")
        c.wait("pe", tok_yt)
        t_ep_rd = [None, None]
        t_x_rd = [None, None]
        t_yev = [None, None]
        t_x2_wr = [None, None]
        for ii, ti in enumerate(tiles):
            j = ii % 2
            tok_ep = c.dma("sp", sb_ep[j].ap().rearrange("p (m t) -> p m t", m=16),
                           eprime[:, :, ti * TILE:(ti + 1) * TILE].transpose([1, 0, 2]),
                           wait=[t_ep_rd[j]], stream=f"ep{j}")
            tok_x = c.dma("sp", sb_x[j][:], x_src[ti * TILE:(ti + 1) * TILE, :],
                          wait=[t_x_rd[j]], stream=f"x6{j}")
            mm_list = []
            for k in range(16):
                for nb in range(2):
                    mm_list.append(lambda t, j=j, k=k, nb=nb: t.matmul(
                        ps_y[j][:, nb * 512:(nb + 1) * 512],
                        sb_ep[j][:, k * TILE:(k + 1) * TILE],
                        sb_yt[:, k * D + nb * 512: k * D + (nb + 1) * 512],
                        start=(k == 0), stop=(k == 15)))
            t_ymm = c.chain("pe", mm_list, inc="pe", wait=[tok_ep, t_yev[j]])
            t_ep_rd[j] = t_ymm
            t_x2 = c.chain("dve", [lambda v, j=j: v.scalar_tensor_tensor(
                sb_x2[j][:], ps_y[j][:], 0.5, sb_x[j][:], OP.mult, OP.add)],
                inc="dve", wait=[t_ymm, tok_x, t_x2_wr[j]])
            t_x_rd[j] = t_x2
            t_yev[j] = t_x2
            t_wr = c.dma("sp", resid[ti * TILE:(ti + 1) * TILE, :], sb_x2[j][:],
                         wait=[t_x2], stream=f"x2st{j}")
            if not first:
                t_wr = c.dma("sp", y_out[(ti - 1) * TILE:ti * TILE, :],
                             sb_x2[j][:], wait=[t_x2], stream=f"x2st{j}")
            t_x2_wr[j] = t_wr
        c.barrier()
        pd.close()

    def emit_mamba():
        pfx = "m_"
        mes = contextlib.ExitStack()
        sb_dtb = SBp(mes, pfx + "dtb", [NHEADS, 4])
        sb_scol = SBp(mes, pfx + "scol", [TILE, 512])
        sb_bcol = SBp(mes, pfx + "bcol", [TILE, 512])
        sb_escol = SBp(mes, pfx + "escol", [TILE, 512])
        sb_wcol = SBp(mes, pfx + "wcolb", [TILE, 512], BF16)
        sb_lam = SBp(mes, pfx + "lamall", [TILE, 256])
        sb_slast = SBp(mes, pfx + "slast", [1, 512])
        sb_lamrow = SBp(mes, pfx + "lamrow", [1, 512])
        sb_dtm = SBp(mes, pfx + "dtm", [TILE, NHEADS])
        m12 = contextlib.ExitStack()
        sb_sg = SBp(m12, pfx + "sg", [NHEADS, TREAL])
        sb_ufm = SBp(m12, pfx + "ufm", [TILE, 8 * TTOT], BF16)

        m1 = contextlib.ExitStack()
        sb_x2t = [SBp(m1, pfx + f"x2t{j}", [TILE, D]) for j in range(2)]
        sb_xa = [SBp(m1, pfx + f"xa{j}", [TILE, D]) for j in range(2)]
        sb_x2a = [SBp(m1, pfx + f"x2a{j}", [TILE, D]) for j in range(2)]
        sb_up = [SBp(m1, pfx + f"up{j}", [TILE, D], BF16) for j in range(2)]
        sb_mst = [SBp(m1, pfx + f"mst{j}", [TILE, ST_W]) for j in range(2)]
        sb_mst2 = [SBp(m1, pfx + f"mst2{j}", [TILE, ST_W]) for j in range(2)]
        ps_utr = PSp(m1, pfx + "psutr", [TILE, 8 * TILE], BF16)

        t_x2_rd = [None, None]
        t_rp1 = [None, None]
        t_rp2 = [None, None]
        t_x2a_sp = [None, None]
        t_ufm_wr = None
        t_psutr_ev = None
        for ti in range(NTE):
            j = ti % 2
            x2t, xa, x2a, up = sb_x2t[j], sb_xa[j], sb_x2a[j], sb_up[j]
            tok = c.dma("sp", x2t[:], resid[ti * TILE:(ti + 1) * TILE, :],
                        wait=[t_x2_rd[j]], stream=f"mx{j}")
            mu1, rstd1, t_r1 = ln_chain(sb_mst[j], x2t, D,
                                        waits=[tok, t_rp1[j]])
            t_rp1[j] = t_r1
            t_xa = c.chain("dve", [lambda v, o=xa, a=x2t, m=mu1, r=rstd1:
                                   v.tensor_scalar(o[:], a[:], m, r,
                                                   OP.subtract, OP.mult)],
                           inc="dve", wait=[t_r1])
            if ti >= 1:
                t_x2av = c.chain("dve", [lambda v, o=x2a, a=x2t, b=xa:
                                         v.tensor_add(o[:], a[:], b[:])],
                                 inc="dve", wait=[t_x2a_sp[j]])
                t_x2a_sp[j] = c.dma(
                    "sp", x2a_dram[(ti - 1) * TILE:ti * TILE, :], x2a[:],
                    wait=[t_x2av], stream=f"x2asp{j}")
            mu2, rstd2, t_r2 = ln_chain(sb_mst2[j], xa, D,
                                        waits=[t_xa, t_rp2[j]])
            t_rp2[j] = t_r2
            t_up = c.chain("dve", [lambda v, o=up, a=xa, m=mu2, r=rstd2:
                                   v.tensor_scalar(o[:], a[:], m, r,
                                                   OP.subtract, OP.mult)],
                           inc="dve", wait=[t_r2])
            t_x2_rd[j] = t_up
            t_tr = c.chain("pe", [
                (lambda t, k=k, i_=up: t.transpose(
                    ps_utr[:, k * TILE:(k + 1) * TILE],
                    i_[:, k * TILE:(k + 1) * TILE], sb_ibf[:]))
                for k in range(8)
            ], inc="pe", wait=[t_up, t_psutr_ev])
            ev = []
            for k in range(8):
                ev.append(lambda v, k=k, ti=ti: v.scalar_tensor_tensor(
                    sb_ufm[:, k * TTOT + ti * TILE: k * TTOT + (ti + 1) * TILE],
                    ps_utr[:, k * TILE:(k + 1) * TILE],
                    ln_w_cols_sb[:, k:k + 1],
                    ln_b_cols_sb[:, k:k + 1].broadcast_to([TILE, TILE]),
                    OP.mult, OP.add))
            t_ufm_wr = c.chain("dve", ev, inc="dve", wait=[t_tr])
            t_psutr_ev = t_ufm_wr
        t_pm = c.chain("dve", [
            (lambda v, k=k: v.tensor_tensor(
                sb_ufm[:, k * TTOT + 125: k * TTOT + 128],
                sb_ufm[:, k * TTOT + 125: k * TTOT + 128],
                sb_pmask[:, 0:3], OP.mult))
            for k in range(8)
        ], inc="dve", wait=[t_ufm_wr])
        c.barrier()
        if dev_stop == "mamba":
            c.dma("act", taps["t_ufm"][:], sb_ufm[:], stream="tap")
        m1.close()

        m2 = contextlib.ExitStack()
        sb_wz = SBp(m2, pfx + "wz", [TILE, 8 * DIN], BF16)
        sb_zs = [SBp(m2, pfx + f"zs{j}", [TILE, DIN], BF16) for j in range(2)]
        tok_wz = c.dma("pool", sb_wz.ap().rearrange("p (k n) -> p k n", k=8),
                       W_inT[:, 0:DIN].rearrange("(k p) n -> p k n", p=TILE),
                       stream="wz")
        m2a = contextlib.ExitStack()
        ps_z = PSp(m2a, pfx + "psz", [TILE, DIN])
        t_zev = [None, None]
        t_zsp = [None, None]
        t_zmm_prev = None
        c.wait("pe", tok_wz)
        t_zevh = [None, None]
        for ti in range(1, NTE):
            j = ti % 2
            t_si_h = [None, None]
            for half in range(2):
                mm_list = []
                for k in range(8):
                    for nb in (2 * half, 2 * half + 1):
                        mm_list.append(lambda t, ti=ti, k=k, nb=nb: t.matmul(
                            ps_z[:, nb * 512:(nb + 1) * 512],
                            sb_ufm[:, k * TTOT + ti * TILE:
                                   k * TTOT + (ti + 1) * TILE],
                            sb_wz[:, k * DIN + nb * 512: k * DIN + (nb + 1) * 512],
                            start=(k == 0), stop=(k == 7)))
                t_zmm = c.chain("pe", mm_list, inc="pe", wait=[t_zevh[half]])
                t_si_h[half] = c.chain("act", [lambda s, j=j, half=half:
                                               s.activation(
                    sb_zs[j][:, half * 1024:(half + 1) * 1024],
                    ps_z[:, half * 1024:(half + 1) * 1024], AF.Silu)],
                    inc="act", wait=[t_zmm, t_zsp[j]])
                t_zevh[half] = t_si_h[half]
            t_zsp[j] = c.dma("sp", z_dram[(ti - 1) * TILE:ti * TILE, :],
                             sb_zs[j][:], wait=[t_si_h[0], t_si_h[1]],
                             stream=f"zsp{j}")
        c.barrier()
        m2a.close()

        m2b = contextlib.ExitStack()
        sb_wxbc = [SBp(m2b, pfx + f"wxbc{j}", [TILE, 8 * TILE], BF16)
                   for j in range(2)]
        sb_xbcst = [SBp(m2b, pfx + f"xbcst{j}", [TILE, TTOT], BF16)
                    for j in range(2)]
        ps_xbc = PSp(m2b, pfx + "psxbc", [TILE, 2560])
        TBW = [(0, 512), (512, 512), (1024, 512), (1536, 512), (2048, 128)]
        tok_wx = [None, None]
        tok_wx[0] = c.dma("pool",
                          sb_wxbc[0].ap().rearrange("p (k n) -> p k n", k=8),
                          W_inT[:, DIN:DIN + TILE].rearrange("(k p) n -> p k n",
                                                             p=TILE),
                          stream="wxbc0")
        t_xbev = [None, None]
        t_xbev_last = None
        t_xbsp = [None, None]
        t_xbmm = [None, None]
        for mi in range(17):
            j = mi % 2
            if mi + 1 < 17:
                tok_wx[(mi + 1) % 2] = c.dma(
                    "pool",
                    sb_wxbc[(mi + 1) % 2].ap().rearrange("p (k n) -> p k n", k=8),
                    W_inT[:, DIN + (mi + 1) * TILE: DIN + (mi + 2) * TILE
                          ].rearrange("(k p) n -> p k n", p=TILE),
                    wait=[t_xbmm[(mi + 1) % 2]], stream=f"wxbc{(mi + 1) % 2}")
            mm_list = []
            for tb, (c0, w) in enumerate(TBW):
                for k in range(8):
                    mm_list.append(lambda t, mi=mi, j=j, k=k, c0=c0, w=w, tb=tb:
                                   t.matmul(
                        ps_xbc[:, tb * 512: tb * 512 + w],
                        sb_wxbc[j][:, k * TILE:(k + 1) * TILE],
                        sb_ufm[:, k * TTOT + c0: k * TTOT + c0 + w],
                        start=(k == 0), stop=(k == 7)))
            t_mm = c.chain("pe", mm_list, inc="pe",
                           wait=[tok_wx[j], t_xbev_last, t_pm])
            t_xbmm[j] = t_mm
            t_ev = c.chain("dve", [lambda v, j=j: v.tensor_copy(
                sb_xbcst[j][:], ps_xbc[:, 0:TTOT])], inc="dve",
                wait=[t_mm, t_xbsp[j]])
            t_xbev[j] = t_ev
            t_xbev_last = t_ev
            t_xbsp[j] = c.dma("sp", xbcpre[mi, :, :],
                              sb_xbcst[j][:, 125:TTOT], wait=[t_ev],
                              stream=f"xbcsp{j}")
        c.barrier()
        m2b.close()

        m2c = contextlib.ExitStack()
        sb_wdt = SBp(m2c, pfx + "wdt", [TILE, 8 * 32], BF16)
        ps_dt = PSp(m2c, pfx + "psdt", [32, 512])
        tok_wdt = c.dma("pool", sb_wdt.ap().rearrange("p (k n) -> p k n", k=8),
                        W_inT[:, DPROJ - 32:DPROJ].rearrange("(k p) n -> p k n",
                                                             p=TILE),
                        stream="wdt")
        tok_dtb = c.dma("sp", sb_dtb[:, 0:1], dt_bias[:], stream="dtb")
        tok_alog = c.dma("sp", sb_dtb[:, 1:2], A_log[:], stream="dtb")
        t_ndtb = c.chain("dve", [lambda v: v.tensor_scalar_mul(
            sb_dtb[:, 2:3], sb_dtb[:, 0:1], -1.0)], inc="dve",
            wait=[tok_dtb, tok_alog])
        t_eA = c.chain("act", [lambda s: s.activation(
            sb_dtb[:, 3:4], sb_dtb[:, 1:2], AF.Exp)], inc="act", wait=[t_ndtb])
        t_sg_prev = None
        c.wait("pe", tok_wdt)
        for tb in range(4):
            mm_list = []
            for k in range(8):
                mm_list.append(lambda t, tb=tb, k=k: t.matmul(
                    ps_dt[:, 0:512],
                    sb_wdt[:, k * 32:(k + 1) * 32],
                    sb_ufm[:, k * TTOT + TILE + tb * 512:
                           k * TTOT + TILE + (tb + 1) * 512],
                    start=(k == 0), stop=(k == 7)))
            t_mm = c.chain("pe", mm_list, inc="pe", wait=[t_sg_prev])
            t_sg_prev = c.chain("act", [lambda s, tb=tb: s.activation(
                sb_sg[:, tb * 512:(tb + 1) * 512], ps_dt[:], AF.Sigmoid,
                scale=-1.0, bias=sb_dtb[:, 2:3])], inc="act",
                wait=[t_mm, t_ndtb])
        c.barrier()
        m2c.close()
        m2.close()
        if dev_stop == "mamba":
            c.dma("act", taps["t_xbc"][:], xbcpre[:], stream="tap")
            c.dma("act", taps["t_z"][:], z_dram[:], stream="tap")
            c.dma("act", taps["t_sg"][:], sb_sg[:], stream="tap")
            c.dma("act", taps["t_x2a"][:], x2a_dram[:], stream="tap")

        m3 = contextlib.ExitStack()
        sb_l = SBp(m3, pfx + "l", [NHEADS, TREAL])
        sb_dtA = SBp(m3, pfx + "dtA", [NHEADS, TREAL])
        sb_lndt = SBp(m3, pfx + "lndt", [NHEADS, TREAL])
        sb_dtAc = SBp(m3, pfx + "dtAc", [TILE, 512])
        sb_lndtc = SBp(m3, pfx + "lndtc", [TILE, 512])
        ps_m3 = PSp(m3, pfx + "psm3", [TILE, 512])
        ps_m3b = PSp(m3, pfx + "psm3b", [TILE, 512])

        t_l = c.chain("act", [lambda s: s.activation(sb_l[:], sb_sg[:], AF.Ln)],
                      inc="act")
        t_dtA = c.chain("dve", [lambda v: v.tensor_scalar_mul(
            sb_dtA[:], sb_l[:], sb_dtb[:, 3:4])], inc="dve",
            wait=[t_l, t_eA])
        t_lndt = c.chain("act", [lambda s: s.activation(
            sb_lndt[:], sb_l[:], AF.Ln, scale=-1.0)], inc="act", wait=[t_dtA])
        t_tr1 = c.chain("pe", [
            (lambda t, cc=cc: t.transpose(
                ps_m3[:, cc * 32:(cc + 1) * 32],
                sb_dtA[:, cc * TILE:(cc + 1) * TILE], sb_if32[0:32, 0:32]))
            for cc in range(16)
        ], inc="pe", wait=[t_dtA])
        t_dtac = c.chain("dve", [lambda v: v.tensor_copy(sb_dtAc[:], ps_m3[:])],
                         inc="dve", wait=[t_tr1])
        t_tr2 = c.chain("pe", [
            (lambda t, cc=cc: t.transpose(
                ps_m3b[:, cc * 32:(cc + 1) * 32],
                sb_lndt[:, cc * TILE:(cc + 1) * TILE], sb_if32[0:32, 0:32]))
            for cc in range(16)
        ], inc="pe", wait=[t_lndt])
        t_lndc = c.chain("dve", [lambda v: v.tensor_copy(sb_lndtc[:], ps_m3b[:])],
                         inc="dve", wait=[t_tr2])
        t_mm_s = c.chain("pe", [
            lambda t: t.matmul(ps_m3[:], sb_maskT[:], sb_dtAc[:],
                               start=True, stop=True)],
            inc="pe", wait=[t_dtac])
        t_scol = c.chain("dve", [lambda v: v.tensor_copy(sb_scol[:], ps_m3[:])],
                         inc="dve", wait=[t_mm_s])
        t_mm_sl = c.chain("pe", [
            lambda t: t.matmul(ps_m3b[0:1, :], sb_onescol[:], sb_dtAc[:],
                               start=True, stop=True)],
            inc="pe", wait=[t_scol])
        t_slast = c.chain("dve", [lambda v: v.tensor_copy(
            sb_slast[:], ps_m3b[0:1, :])], inc="dve", wait=[t_mm_sl])
        t_bcol = c.chain("dve", [lambda v: v.tensor_sub(
            sb_bcol[:], sb_lndtc[:], sb_scol[:])], inc="dve",
            wait=[t_lndc, t_scol])
        t_escol = c.chain("act", [lambda s: s.activation(
            sb_escol[:], sb_scol[:], AF.Exp)], inc="act", wait=[t_scol])
        t_lamr = c.chain("act", [lambda s: s.activation(
            sb_lamrow[:], sb_slast[:], AF.Exp)], inc="act", wait=[t_slast])
        t_mm_w = c.chain("pe", [
            lambda t: t.matmul(ps_m3[:], sb_crf[0:1, 0:TILE], sb_slast[:],
                               start=True, stop=False),
            lambda t: t.matmul(ps_m3[:], sb_if32[:], sb_bcol[:],
                               start=False, stop=True)],
            inc="pe", wait=[t_slast, t_bcol, t_scol])
        t_wcol = c.chain("act", [lambda s: s.activation(
            sb_wcol[:], ps_m3[:], AF.Exp)], inc="act", wait=[t_mm_w])
        lam_even = sb_lamrow.ap().rearrange("o (cc h) -> o cc h", h=32)[:, :, 0:16]
        lam_odd = sb_lamrow.ap().rearrange("o (cc h) -> o cc h", h=32)[:, :, 16:32]
        t_mm_lam = c.chain("pe", [
            lambda t: t.matmul(ps_m3b[:, 0:256], sb_crf[0:1, TILE:2 * TILE],
                               lam_even, start=True, stop=False),
            lambda t: t.matmul(ps_m3b[:, 0:256], sb_crf[0:1, 2 * TILE:3 * TILE],
                               lam_odd, start=False, stop=True)],
            inc="pe", wait=[t_lamr])
        t_lam = c.chain("dve", [lambda v: v.tensor_copy(
            sb_lam[:], ps_m3b[:, 0:256])], inc="dve", wait=[t_mm_lam])
        t_mm_d = c.chain("pe", [
            lambda t: t.matmul(ps_m3[:, 0:NHEADS], sb_crf[0:1, 0:TILE],
                               sb_dp[:], start=True, stop=True)],
            inc="pe", wait=[t_wcol])
        t_dcol = c.chain("dve", [lambda v: v.tensor_copy(
            sb_dtm[:], ps_m3[:, 0:NHEADS])], inc="dve", wait=[t_mm_d])
        c.barrier()
        if dev_stop == "mamba":
            c.dma("act", taps["t_vecs"][:, 0:512], sb_scol[:], stream="tap")
            c.dma("act", taps["t_vecs"][:, 512:1024], sb_bcol[:], stream="tap")
            c.dma("act", taps["t_vecs"][:, 1024:1536], sb_escol[:], stream="tap")
            c.chain("dve", [lambda v: v.tensor_copy(
                sb_trash[:, 0:512], sb_wcol[:])], inc="dve")
            c.dma("act", taps["t_vecs"][:, 1536:2048], sb_trash[:, 0:512],
                  wait=[("dve", c.cnt["dve"])], stream="tap")
            c.dma("act", taps["t_vecs"][:, 2048:2304], sb_lam[:], stream="tap")
            c.dma("act", taps["t_vecs"][:, 2304:2336], sb_dtm[:], stream="tap")
        m3.close()
        m12.close()

        m4 = contextlib.ExitStack()
        sb_xsf = SBp(m4, pfx + "xsf", [TILE, 17 * DIN], BF16)
        sb_c0b = SBp(m4, pfx + "c0b", [DSTATE, TREAL], BF16)
        sb_cw = SBp(m4, pfx + "cw", [TILE, 17 * 4])
        sb_cb = SBp(m4, pfx + "cb", [TILE, 17])
        sb_xstm = [SBp(m4, pfx + f"xstm{j}", [TILE, DIN + DSTATE], BF16)
                   for j in range(2)]
        sb_slocb = [SBp(m4, pfx + f"slocb{j}", [TILE, 1024], BF16)
                    for j in range(2)]
        sb_sA = [SBp(m4, pfx + f"sA{j}", [TILE, 1024]) for j in range(2)]
        sb_stmp = SBp(m4, pfx + "stmp", [TILE, 1024])
        m4p = contextlib.ExitStack()
        sb_xbr = [SBp(m4p, pfx + f"xbr{j}", [TILE, TREAL + 3], BF16)
                  for j in range(2)]
        sb_acc = [SBp(m4p, pfx + "acc0", [TILE, TREAL])] * 2
        tok_cw = c.dma("sp", sb_cw[:], conv_w[:], stream="cw")
        tok_cb = c.dma("sp", sb_cb[:], conv_b[:], stream="cw")
        t_xbr_rd = [None, None]
        t_acc_rd = [None, None]
        t_conv_done = []
        for r in range(17):
            j = r % 2
            tok_x = c.dma("sp", sb_xbr[j][:], xbcpre[r, :, :],
                          wait=[t_xbr_rd[j]], stream=f"xbr{j}")
            ops = [lambda v, j=j, r=r: v.tensor_scalar_mul(
                sb_acc[j][:], sb_xbr[j][:, 0:TREAL], sb_cw[:, r * 4:r * 4 + 1])]
            for k in range(1, 4):
                ops.append(lambda v, j=j, r=r, k=k: v.scalar_tensor_tensor(
                    sb_acc[j][:], sb_xbr[j][:, k:TREAL + k],
                    sb_cw[:, r * 4 + k:r * 4 + k + 1], sb_acc[j][:],
                    OP.mult, OP.add))
            t_cv = c.chain("dve", ops, inc="dve",
                           wait=[tok_x, tok_cw, t_acc_rd[0]])
            t_xbr_rd[j] = t_cv
            t_si = c.chain("act", [lambda s, j=j, r=r: s.activation(
                sb_xsf[:, r * DIN:(r + 1) * DIN], sb_acc[j][:], AF.Silu,
                bias=sb_cb[:, r:r + 1])], inc="act",
                wait=[t_cv, tok_cb])
            t_acc_rd[0] = t_si
            t_conv_done.append(t_si)
        t_c0b = c.dma("sp", sb_c0b[:],
                      sb_xsf[DSTATE:2 * DSTATE, 16 * DIN:17 * DIN],
                      wait=[t_conv_done[16]], stream="ccp")
        c.barrier()
        m4p.close()
        if dev_stop == "mamba":
            c.dma("act", taps["t_xsf"][:], sb_xsf[:], stream="tap")

        m4a = contextlib.ExitStack()
        sb_xw = [SBp(m4a, pfx + f"xw{j}", [TILE, DIN], BF16) for j in range(2)]
        ps_xta = PSp(m4a, pfx + "psxta", [TILE, DIN + DSTATE], BF16)
        ps_S = PSp(m4a, pfx + "psS", [TILE, 1024])

        t_init0 = c.chain("pool", [lambda g: g.memset(sb_sA[0][:], 0.0)],
                          inc="pool")
        t_slsp = [None, None]
        t_xstm_rd = [None, None]
        t_xw_rd = [None, None]
        t_psxta_ev = None
        t_psS_rd = None
        t_rec = t_init0
        for cc in range(16):
            j = cc % 2
            trs = [
                (lambda t, r=r, cc=cc: t.transpose(
                    ps_xta[:, r * TILE:(r + 1) * TILE],
                    sb_xsf[:, r * DIN + cc * TILE: r * DIN + (cc + 1) * TILE],
                    sb_ibf[:]))
                for r in range(16)
            ]
            trs.append(lambda t, cc=cc: t.transpose(
                ps_xta[:, DIN:DIN + DSTATE],
                sb_xsf[0:DSTATE, 16 * DIN + cc * TILE: 16 * DIN + (cc + 1) * TILE],
                sb_ibf[0:DSTATE, 0:DSTATE]))
            t_tr = c.chain("pe", trs, inc="pe",
                           wait=[t_conv_done[-1], t_psxta_ev])
            t_ev = c.chain("dve", [lambda v, j=j: v.tensor_copy(
                sb_xstm[j][:], ps_xta[:])], inc="dve",
                wait=[t_tr, t_xstm_rd[j]])
            t_psxta_ev = t_ev
            t_xwv = c.chain("dve", [lambda v, j=j, cc=cc: v.tensor_tensor(
                sb_xw[j].ap().rearrange("p (h q) -> p h q", h=32),
                sb_xstm[j][:, 0:DIN].rearrange("p (h q) -> p h q", h=32),
                sb_wcol[:, cc * 32:(cc + 1) * 32].unsqueeze(2
                    ).broadcast_to([TILE, 32, HEADDIM]),
                OP.mult)], inc="dve", wait=[t_ev, t_xw_rd[j], t_wcol])
            mm_list = []
            for h in range(NHEADS):
                h2, hp = h // 16, h % 16
                mm_list.append(lambda t, j=j, h2=h2, hp=hp, h=h: t.matmul(
                    ps_S[h2 * 64:(h2 + 1) * 64, hp * 64:(hp + 1) * 64],
                    sb_xstm[j][:, DIN:DIN + DSTATE],
                    sb_xw[j][:, h * HEADDIM:(h + 1) * HEADDIM],
                    start=True, stop=True,
                    tile_position=(0, 64 * h2)))
            t_smm = c.chain("pe", mm_list, inc="pe", wait=[t_xwv, t_psS_rd])
            t_xstm_rd[j] = t_smm
            t_xw_rd[j] = t_smm
            t_sev = c.chain("dve", [lambda v, cc=cc, j=j: v.tensor_copy(
                sb_slocb[j][:], ps_S[:])],
                inc="dve", wait=[t_smm, t_slsp[j]])
            t_slsp[j] = c.dma("sp", sloc_dram[cc, :, :], sb_slocb[j][:],
                              wait=[t_sev], stream=f"slsp{j}")
            cur, nxt = cc % 2, (cc + 1) % 2
            t_tmp = c.chain("dve", [lambda v, cur=cur, cc=cc: v.tensor_tensor(
                sb_stmp.ap().rearrange("p (a b) -> p a b", a=16),
                sb_sA[cur].ap().rearrange("p (a b) -> p a b", a=16),
                sb_lam[:, cc * 16:(cc + 1) * 16].unsqueeze(2
                    ).broadcast_to([TILE, 16, 64]),
                OP.mult)], inc="dve", wait=[t_rec, t_lam])
            t_rec = c.chain("dve", [lambda v, nxt=nxt: v.scalar_tensor_tensor(
                sb_sA[nxt][:], ps_S[:], 0.0, sb_stmp[:], OP.add, OP.add)],
                inc="dve", wait=[t_tmp])
            t_psS_rd = t_rec
        if dev_stop == "mamba":
            c.dma("act", taps["t_sA"][:], sb_sA[0][:], wait=[t_rec],
                  stream="tap")
        t_stin = c.dma("sp", cc_st_in[:], sb_sA[0][:], wait=[t_rec],
                       stream="stio")
        t_agst = c.chain("pool", [lambda g: g.collective_compute(
            "AllGather", OP.bypass,
            replica_groups=[[0, 1], [2, 3], [4, 5], [6, 7]],
            ins=[cc_st_in[:]], outs=[cc_st_out[:]])], inc="cc",
            wait=[t_stin])
        t_stld = c.dma("sp", sb_stmp[:], cc_st_out[0:TILE, :], wait=[t_agst],
                       stream="stio")
        t_sB = c.chain("dve", [lambda v: v.tensor_scalar_mul(
            sb_sA[0][:], sb_stmp[:], sb_oddmask[:])], inc="dve",
            wait=[t_stld])
        c.barrier()
        if dev_stop == "mamba":
            c.dma("act", taps["t_sloc"][0:TILE, 0:1024],
                  sloc_dram[0, :, :], stream="tap")
        m4a.close()

        m4b = contextlib.ExitStack()
        sb_g0m = SBp(m4b, pfx + "g0m", [TILE, 4 * TILE])
        sb_sr4 = [SBp(m4b, pfx + f"sr4_{j}", [TILE, 512]) for j in range(2)]
        sb_bm4 = [SBp(m4b, pfx + "bm4_0", [TILE, 512])] * 2
        sb_gp4 = [SBp(m4b, pfx + f"gp4_{j}", [TILE, 512]) for j in range(2)]
        sb_gf4 = [SBp(m4b, pfx + f"gf4_{j}", [TILE, 512], BF16)
                  for j in range(2)]
        sb_sBb = SBp(m4b, pfx + "sBb", [TILE, 1024], BF16)
        sb_yg = SBp(m4b, pfx + "yg", [TILE, DIN])
        sb_dxs = SBp(m4b, pfx + "dxs", [TILE, DIN], BF16)
        sb_ybf = SBp(m4b, pfx + "ybf", [TILE, DIN], BF16)
        sb_yfm = sb_ybf
        sb_zld = [SBp(m4b, pfx + "zld0", [TILE, DIN], BF16)] * 2
        sb_x2al = [SBp(m4b, pfx + "x2al0", [TILE, D])] * 2
        sb_x3 = [SBp(m4b, pfx + "x30", [TILE, D])] * 2
        sb_wout = SBp(m4b, pfx + "wout", [TILE, 16 * D], BF16)
        sb_rms = SBp(m4b, pfx + "rms", [TILE, 4])
        ps_A = PSp(m4b, pfx + "psA", [TILE, DIN])
        ps_bt = PSp(m4b, pfx + "psbt", [TILE, DIN], BF16)
        ps_misc = PSp(m4b, pfx + "psmisc", [TILE, 1024])

        tok_wo = c.dma("pool", sb_wout.ap().rearrange("p (k n) -> p k n", k=16),
                       W_outT.rearrange("(k p) n -> p k n", p=TILE),
                       stream="wout")
        t_zld_rd = [None, None]
        t_x2a_rd = [None, None]
        t_x3_wr = [None, None]
        t_srep_rd = [None, None]
        t_gpre_rd = [None, None]
        t_gful_rd = [None, None]
        t_psbt_ev = None
        t_g0m_rd = None
        t_recB = t_sB
        t_sBb_rd = None
        t_yg_done = None
        t_psA_free = None
        t_ybf_rd = None
        t_yfm_rd = None
        t_slB_rd = [None, None]
        t_dxs_rd = None
        t_dtr_last = None
        for cc in range(16):
            j = cc % 2
            tok_z = c.dma("sp", sb_zld[j][:],
                          z_dram[cc * TILE:(cc + 1) * TILE, :],
                          wait=[t_zld_rd[0]], stream="zld")
            tok_xa = c.dma("sp", sb_x2al[j][:],
                           x2a_dram[cc * TILE:(cc + 1) * TILE, :],
                           wait=[t_x2a_rd[0]], stream="x2al")
            trs = [
                (lambda t, r=r, cc=cc: t.transpose(
                    ps_bt[:, r * TILE:(r + 1) * TILE],
                    sb_xsf[:, r * DIN + cc * TILE: r * DIN + (cc + 1) * TILE],
                    sb_ibf[:]))
                for r in range(16)
            ]
            t_tr = c.chain("pe", trs, inc="pe", wait=[t_psbt_ev])
            t_ev = c.chain("dve", [lambda v, j=j: v.tensor_copy(
                sb_xstm[j][:, 0:DIN], ps_bt[:, 0:DIN])], inc="dve",
                wait=[t_tr, t_xstm_rd[j]])
            t_psbt_ev = t_ev
            t_sbb = c.chain("dve", [lambda v, cur=cc % 2: v.tensor_copy(
                sb_sBb[:], sb_sA[cur][:])], inc="dve",
                wait=[t_recB, t_sBb_rd])
            mm_list = []
            for h2 in range(2):
                for nb in range(2):
                    lhs = ((lambda t, cc=cc: sb_c0b[:, cc * TILE:(cc + 1) * TILE])
                           if h2 == 0 else
                           (lambda t, cc=cc: sb_xsf[DSTATE:2 * DSTATE,
                                16 * DIN + cc * TILE: 16 * DIN + (cc + 1) * TILE]))
                    mm_list.append(
                        lambda t, h2=h2, nb=nb, lhs=lhs, cc=cc: t.matmul(
                            ps_A[:, h2 * 1024 + nb * 512:
                                 h2 * 1024 + (nb + 1) * 512],
                            lhs(t),
                            sb_sBb[h2 * 64:(h2 + 1) * 64,
                                   nb * 512:(nb + 1) * 512],
                            start=True, stop=True))
            t_yi = c.chain("pe", mm_list, inc="pe", wait=[t_sbb, t_psA_free])
            t_es = c.chain("dve", [lambda v, cc=cc: v.tensor_tensor(
                ps_A.ap().rearrange("p (h q) -> p h q", h=32),
                ps_A.ap().rearrange("p (h q) -> p h q", h=32),
                sb_escol[:, cc * 32:(cc + 1) * 32].unsqueeze(2
                    ).broadcast_to([TILE, 32, HEADDIM]),
                OP.mult)], inc="dve", wait=[t_yi, t_escol])
            tok_sl = c.dma("sp", sb_slocb[j][:], sloc_dram[cc, :, :],
                           wait=[t_slB_rd[j]], stream=f"slsp{j}")
            cur, nxt = cc % 2, (cc + 1) % 2
            t_tmpB = c.chain("dve", [lambda v, cur=cur, cc=cc: v.tensor_tensor(
                sb_stmp.ap().rearrange("p (a b) -> p a b", a=16),
                sb_sA[cur].ap().rearrange("p (a b) -> p a b", a=16),
                sb_lam[:, cc * 16:(cc + 1) * 16].unsqueeze(2
                    ).broadcast_to([TILE, 16, 64]),
                OP.mult)], inc="dve", wait=[t_yi])
            t_recB = c.chain("dve", [lambda v, nxt=nxt, j=j: v.tensor_tensor(
                sb_sA[nxt][:], sb_slocb[j][:],
                sb_stmp[:], OP.add)], inc="dve", wait=[t_tmpB, tok_sl])
            t_slB_rd[j] = t_recB
            t_sBb_rd = t_yi
            t_g0 = c.chain("pe", [lambda t, cc=cc: t.matmul(
                ps_misc[:, 512:512 + TILE],
                sb_xsf[0:DSTATE, 16 * DIN + cc * TILE: 16 * DIN + (cc + 1) * TILE],
                sb_c0b[:, cc * TILE:(cc + 1) * TILE],
                start=True, stop=True)], inc="pe",
                wait=[t_g0m_rd, t_es])
            t_g0m = c.chain("dve", [lambda v: v.tensor_copy(
                sb_g0m.ap().rearrange("p (a b) -> p a b", a=4),
                ps_misc[:, 512:512 + TILE].unsqueeze(1
                    ).broadcast_to([TILE, 4, TILE]))],
                inc="dve", wait=[t_g0])
            t_intra_last = None
            for g4 in range(8):
                jh = g4 % 2
                h0 = g4 * 4
                t_srep = c.chain("dve", [
                    lambda v, jh=jh, cc=cc, h0=h0: v.tensor_copy(
                        sb_sr4[jh].ap().rearrange("p (a b) -> p a b", a=4),
                        sb_scol[:, cc * 32 + h0: cc * 32 + h0 + 4
                                ].unsqueeze(2).broadcast_to([TILE, 4, TILE])),
                    lambda v, jh=jh, cc=cc, h0=h0: v.tensor_copy(
                        sb_bm4[jh].ap().rearrange("p (a b) -> p a b", a=4),
                        sb_bcol[:, cc * 32 + h0: cc * 32 + h0 + 4
                                ].unsqueeze(2).broadcast_to([TILE, 4, TILE])),
                ], inc="dve", wait=[t_srep_rd[jh], t_dtr_last])
                pe_ops = []
                for hh in range(4):
                    pe_ops.append(lambda t, jh=jh, hh=hh: t.matmul(
                        ps_misc[:, jh * 512 + hh * TILE: jh * 512 + (hh + 1) * TILE],
                        sb_sr4[jh][:, hh * TILE:(hh + 1) * TILE], sb_if32[:],
                        is_transpose=True, start=(hh == 0), stop=False,
                        skip_group_check=(hh != 0)))
                pe_ops.append(lambda t, jh=jh: t.matmul(
                    ps_misc[:, jh * 512:(jh + 1) * 512], sb_if32[:],
                    sb_maskneg[:], start=False, stop=False,
                    skip_group_check=True))
                pe_ops.append(lambda t, jh=jh: t.matmul(
                    ps_misc[:, jh * 512:(jh + 1) * 512], sb_if32[:],
                    sb_bm4[jh][:], start=False, stop=True))
                t_dtr = c.chain("pe", pe_ops, inc="pe",
                                wait=[t_srep, t_gpre_rd[jh], t_g0m])
                t_srep_rd[jh] = t_dtr
                t_dtr_last = t_dtr
                t_gp = c.chain("act", [lambda s, jh=jh: s.activation(
                    sb_gp4[jh][:], ps_misc[:, jh * 512:(jh + 1) * 512],
                    AF.Exp)], inc="act", wait=[t_dtr, t_gful_rd[jh]])
                t_gpre_rd[jh] = t_gp
                t_gf = c.chain("pool", [lambda g, jh=jh: g.tensor_tensor(
                    sb_gf4[jh][:], sb_gp4[jh][:], sb_g0m[:],
                    OP.mult)], inc="pool", wait=[t_gp, t_g0m, t_gful_rd[jh]])
                mi_ops = []
                for hh in range(4):
                    mi_ops.append(lambda t, jh=jh, j=j, h0=h0, hh=hh: t.matmul(
                        ps_A[:, (h0 + hh) * HEADDIM:(h0 + hh + 1) * HEADDIM],
                        sb_gf4[jh][:, hh * TILE:(hh + 1) * TILE],
                        sb_xstm[j][:, (h0 + hh) * HEADDIM:
                                 (h0 + hh + 1) * HEADDIM],
                        start=False, stop=False, skip_group_check=True))
                t_mi = c.chain("pe", mi_ops, inc="pe",
                               wait=[t_gf, t_es, t_ev])
                t_gful_rd[jh] = t_mi
                t_intra_last = t_mi
            t_g0m_rd = t_intra_last
            t_xstm_rd[j] = t_intra_last
            t_dxs = c.chain("dve", [lambda v, j=j: v.tensor_tensor(
                sb_dxs.ap().rearrange("p (h q) -> p h q", h=NHEADS),
                sb_xstm[j][:, 0:DIN].rearrange("p (h q) -> p h q", h=NHEADS),
                sb_dtm.ap().unsqueeze(2).broadcast_to([TILE, NHEADS, HEADDIM]),
                OP.mult)], inc="dve", wait=[t_dcol, t_ev, t_dxs_rd])
            t_yg = c.chain("dve", [
                lambda v: v.scalar_tensor_tensor(
                    sb_yg[:], ps_A[:], 0.0, sb_dxs[:], OP.add, OP.add),
                lambda v, j=j: v.tensor_mul(sb_yg[:], sb_yg[:], sb_zld[j][:]),
            ], inc="dve", wait=[t_intra_last, tok_z, t_yg_done, t_dxs])
            t_dxs_rd = t_yg
            t_psA_free = t_yg
            t_sq = c.chain("act", [lambda s: s.activation(
                sb_trash[:, 0:DIN], sb_yg[:], AF.Square,
                accum_out=sb_rms[:, 0:1])], inc="act", wait=[t_yg])
            t_rstd = c.serial_chain("act", [
                lambda s: s.activation(sb_rms[:, 1:2], sb_rms[:, 0:1],
                                       AF.Ln, scale=1.0 / DIN, bias=EPS),
                lambda s: s.activation(sb_rms[:, 2:3], sb_rms[:, 1:2],
                                       AF.Exp, scale=-0.5),
            ], wait=[t_sq])
            t_ybf = c.chain("dve", [lambda v: v.scalar_tensor_tensor(
                sb_ybf[:], sb_yg[:], sb_rms[:, 2:3], norm_w_rep_sb[:],
                OP.mult, OP.mult)], inc="dve", wait=[t_rstd, t_ybf_rd])
            t_yg_done = t_ybf
            if dev_stop == "mamba" and cc == 0:
                t_yg_done = c.dma("act", taps["t_yg0"][:], sb_yg[:],
                                  wait=[t_ybf], stream="tap")
                c.dma("act", taps["t_ybf0"][:], sb_ybf[:], wait=[t_ybf],
                      stream="tap")
            trs = [
                (lambda t, r=r: t.transpose(
                    ps_bt[:, r * TILE:(r + 1) * TILE],
                    sb_ybf[:, r * TILE:(r + 1) * TILE], sb_ibf[:]))
                for r in range(16)
            ]
            t_ytr = c.chain("pe", trs, inc="pe", wait=[t_ybf, t_psbt_ev])
            t_ybf_rd = t_ytr
            t_yfm = c.chain("dve", [
                lambda v: v.tensor_copy(sb_yfm[:], ps_bt[:, 0:DIN]),
            ], inc="dve", wait=[t_ytr, t_yfm_rd])
            t_psbt_ev = t_yfm
            mm_list = []
            for r in range(16):
                for nb in range(2):
                    mm_list.append(lambda t, r=r, nb=nb: t.matmul(
                        ps_A[:, nb * 512:(nb + 1) * 512],
                        sb_yfm[:, r * TILE:(r + 1) * TILE],
                        sb_wout[:, r * D + nb * 512: r * D + (nb + 1) * 512],
                        start=(r == 0), stop=(r == 15)))
            t_wo = c.chain("pe", mm_list, inc="pe",
                           wait=[t_yfm, tok_wo, t_yg])
            t_yfm_rd = t_wo
            t_ybf_rd = t_wo
            if dev_stop == "mamba" and cc == 0:
                t_yfm_rd = c.dma("act", taps["t_yfm0"][:], sb_yfm[:],
                                 wait=[t_yfm], stream="tap")
            t_x3 = c.chain("dve", [lambda v, j=j: v.scalar_tensor_tensor(
                sb_x3[j][:], ps_A[:, 0:D], 0.0, sb_x2al[j][:],
                OP.add, OP.add)], inc="dve",
                wait=[t_wo, tok_xa, t_x3_wr[0]])
            t_psA_free = t_x3
            t_zld_rd[0] = t_yg
            t_x2a_rd[0] = t_x3
            t_x3_wr[0] = c.dma("sp", resid[(cc + 1) * TILE:(cc + 2) * TILE, :],
                               sb_x3[j][:], wait=[t_x3], stream="x3st")
        c.barrier()
        if dev_stop == "mamba":
            for ti in range(NTE):
                c.dma("act", taps["t_x3"][ti * TILE:(ti + 1) * TILE, :],
                      resid[ti * TILE:(ti + 1) * TILE, :], stream="tap")
        m4b.close()
        m4.close()
        mes.close()

    emit_ffn(1)

    if dev_stop == "ffn1":
        for ti in range(NTE):
            c.dma("sp", taps["t_x2"][ti * TILE:(ti + 1) * TILE, :],
                  resid[ti * TILE:(ti + 1) * TILE, :], stream="tap")
        c.barrier()
    elif dev_stop == "mamba":
        emit_mamba()
    else:
        emit_mamba()
        emit_ffn(2)

    _final = [(s, c.cnt[s]) for s in sorted(c.dma_sems) if c.cnt[s] > 0]
    c.on("sp", lambda e, _f=_final: [e.wait_ge(c.sem[s], v) for s, v in _f])

    with nc.Block() as block:
        @block.sync
        def _(sync):
            for f in c.seq["sp"]:
                f(sync)

        @block.tensor
        def _(t):
            for f in c.seq["pe"]:
                f(t)

        @block.scalar
        def _(s):
            for f in c.seq["act"]:
                f(s)

        @block.vector
        def _(v):
            for f in c.seq["dve"]:
                f(v)

        @block.gpsimd
        def _(g):
            for f in c.seq["pool"]:
                f(g)

    es.close()
    return nc


def prep_in_maps(inp):
    x = np.asarray(inp["L0_attn_x"], dtype=np.float32)
    phi_w = np.asarray(inp["L0_ffn_phi_w"], dtype=np.float32)
    phi_b = np.asarray(inp["L0_ffn_phi_b"], dtype=np.float32)
    expert_w = np.asarray(inp["L0_ffn_expert_w"], dtype=np.float32)
    expert_b = np.asarray(inp["L0_ffn_expert_b"], dtype=np.float32)
    W_in = np.asarray(inp["L0_attn_W_in"], dtype=np.float32)
    W_out = np.asarray(inp["L0_attn_W_out"], dtype=np.float32)
    conv_w = np.asarray(inp["L0_attn_conv_w"], dtype=np.float32)
    conv_b = np.asarray(inp["L0_attn_conv_b"], dtype=np.float32)
    dt_bias = np.asarray(inp["L0_attn_dt_bias"], dtype=np.float32)
    A_log = np.asarray(inp["L0_attn_A_log"], dtype=np.float32)
    Dd = np.asarray(inp["L0_attn_D"], dtype=np.float32)
    norm_w = np.asarray(inp["L0_attn_norm_w"], dtype=np.float32)
    ln_w = np.asarray(inp["L0_attn_ln_w"], dtype=np.float32)
    ln_b = np.asarray(inp["L0_attn_ln_b"], dtype=np.float32)

    common = {
        "phi_wT": np.ascontiguousarray(phi_w.T),
        "phi_b": phi_b[None, :],
        "W_inT": np.ascontiguousarray(W_in.T),
        "W_outT": np.ascontiguousarray(W_out.T),
        "conv_w": np.ascontiguousarray(
            conv_w.reshape(17, TILE, 4).transpose(1, 0, 2).reshape(TILE, 68)),
        "conv_b": np.ascontiguousarray(conv_b.reshape(17, TILE).T),
        "dt_bias": dt_bias[:, None],
        "A_log": A_log[:, None],
        "D_row": Dd[None, :],
        "norm_w_rep": np.ascontiguousarray(np.broadcast_to(norm_w, (TILE, DIN))),
        "ln_w_cols": np.ascontiguousarray(ln_w.reshape(8, TILE).T),
        "ln_b_cols": np.ascontiguousarray(ln_b.reshape(8, TILE).T),
        "i_bf": np.eye(TILE, dtype=np.float32).astype(ml_dtypes.bfloat16),
        "i_f32": np.eye(TILE, dtype=np.float32),
        "maskT": np.triu(np.ones((TILE, TILE), dtype=np.float32)),
        "maskneg": np.tile(np.where(np.triu(np.ones((TILE, TILE))) > 0, 0.0,
                                    -30000.0).astype(np.float32), (1, 4)),
        "crows_f": np.concatenate([
            np.ones(TILE, np.float32),
            np.concatenate([np.ones(64, np.float32), np.zeros(64, np.float32)]),
            np.concatenate([np.zeros(64, np.float32), np.ones(64, np.float32)]),
            np.zeros(TILE, np.float32)])[None, :],
        "crows_b": np.ones((1, TILE), np.float32).astype(ml_dtypes.bfloat16),
    }
    maps = []
    for cid in range(NCORES):
        b, h = cid // 2, cid % 2
        real = x[b, h * TREAL:(h + 1) * TREAL]
        prefix = (x[b, TREAL - TILE:TREAL] if h == 1
                  else np.zeros((TILE, D), np.float32))
        m = dict(common)
        m["x_slice"] = np.ascontiguousarray(np.concatenate([prefix, real], 0))
        m["expw"] = np.ascontiguousarray(expert_w[cid * EXP_LOC:(cid + 1) * EXP_LOC])
        m["expb_rep"] = np.ascontiguousarray(
            np.repeat(expert_b[cid * EXP_LOC:(cid + 1) * EXP_LOC], PSLOT, axis=0))
        m["pmask"] = np.full((TILE, 4), float(h), np.float32)
        m["oddmask"] = np.full((TILE, 1), float(h), np.float32)
        maps.append(m)
    return maps


def kernel(**inputs):
    maps = prep_in_maps(inputs)
    nc = build_program(dev_stop=DEV_STOP)
    res = run_bass_kernel_spmd(nc, maps, core_ids=list(range(NCORES)))
    out = np.empty((4, 4096, D), dtype=np.float32)
    key = {"ffn1": "t_x2", "mamba": "t_x3"}.get(DEV_STOP, "y_out")
    for cid in range(NCORES):
        b, h = cid // 2, cid % 2
        r = res.results[cid][key]
        if key in ("t_x2", "t_x3"):
            r = r[TILE:]
        out[b, h * TREAL:(h + 1) * TREAL] = r
    return out
